# version 05 (final)
# baseline (speedup 1.0000x reference)
import os
import numpy as np
import ml_dtypes
import concourse.bass as bass
import concourse.mybir as mybir
from concourse.bass_utils import run_bass_kernel_spmd

F32 = mybir.dt.float32
BF16 = mybir.dt.bfloat16
AF = mybir.ActivationFunctionType
ALU = mybir.AluOpType
AX = mybir.AxisListType
BF = ml_dtypes.bfloat16

T = 2048
D = 1024
KD = D // 128
PP = 16
NN = 16
DTR = 64
TC = 512
NCH = T // TC
WIN_COLS = 5184
EPS = 1e-5

STOP_AFTER = None
TAP_P5G = False
DUMP = None


class _StopEmission(Exception):
    pass


def host_layout(inp):
    g = np.arange(128)
    cols = []
    for p in range(PP):
        cols.extend((g * 16 + p).tolist())
    for p in range(PP):
        for gx in range(32):
            cols.append(2048 + 16 * gx + p)
    for n in range(NN):
        for gx in range(32):
            cols.append(2560 + 16 * gx + n)
    for n in range(NN):
        cols.extend((3072 + g * 16 + n).tolist())
    cols.extend(range(5120, 5184))
    win = np.ascontiguousarray(inp["L0_attn_W_in"][:, cols], dtype=np.float32)

    pg = (np.arange(PP)[:, None] + 16 * g[None, :]).reshape(-1)
    wdt = np.ascontiguousarray(inp["L0_attn_W_dt"][:, pg], dtype=np.float32)
    dtb = np.ascontiguousarray(inp["L0_attn_dt_bias"].reshape(128, 16), np.float32)
    alog = np.ascontiguousarray(inp["L0_attn_A_log"].reshape(128, 256), np.float32)
    dvec = np.ascontiguousarray(inp["L0_attn_D"].reshape(128, 16), np.float32)

    ch = np.array([16 * gx + p for p in range(PP) for gx in range(32)])
    convw = np.ascontiguousarray(
        inp["L0_attn_conv_w"][:, 0, :][ch].reshape(4, 128, 4).transpose(1, 0, 2).reshape(128, 16),
        np.float32)
    convb = np.ascontiguousarray(
        inp["L0_attn_conv_b"][ch].reshape(4, 128).T, np.float32)

    wout = np.ascontiguousarray(inp["L0_attn_W_out"][pg], np.float32)
    ident = np.eye(128, dtype=np.float32)

    rm = inp["L0_ffn_rail_maker"]
    up = inp["L0_ffn_rail_up_proj"]
    bd1 = np.zeros((4, 8, 128, 128), np.float32)
    bd2 = np.zeros((4, 8, 128, 512), np.float32)
    for h in range(4):
        for g8 in range(8):
            for rr in range(8):
                r = 8 * g8 + rr
                bd1[h, g8, 16 * rr:16 * rr + 16, 16 * rr:16 * rr + 16] = rm[h, r]
                bd2[h, g8, 16 * rr:16 * rr + 16, 64 * rr:64 * rr + 64] = up[h, r].T
    wcT = np.ascontiguousarray(inp["L0_ffn_w_cproj"].T, np.float32)

    wqT = np.ascontiguousarray(inp["L1_attn_Wq"].T, np.float32)
    wkT = np.ascontiguousarray(inp["L1_attn_Wk"].T, np.float32)
    wvT = np.ascontiguousarray(inp["L1_attn_Wv"].T, np.float32)
    woT = np.ascontiguousarray(inp["L1_attn_Wo"].T, np.float32)
    qkvT = np.ascontiguousarray(inp["L1_ffn_qkv_weight"].T, np.float32)
    projT = np.ascontiguousarray(inp["L1_ffn_proj_weight"].T, np.float32)
    routT = np.ascontiguousarray(inp["L1_ffn_router_weight"].T, np.float32)
    w1 = np.ascontiguousarray(inp["L1_ffn_moe_w1"].reshape(8 * 1024, 2048), np.float32)
    w2 = np.ascontiguousarray(inp["L1_ffn_moe_w2"].reshape(8 * 2048, 1024), np.float32)
    ln1w = np.ascontiguousarray(inp["L1_ffn_ln1_weight"].reshape(8, 128).T, np.float32)
    ln1b = np.ascontiguousarray(inp["L1_ffn_ln1_bias"].reshape(8, 128).T, np.float32)
    ln2w = np.ascontiguousarray(inp["L1_ffn_ln2_weight"].reshape(8, 128).T, np.float32)
    ln2b = np.ascontiguousarray(inp["L1_ffn_ln2_bias"].reshape(8, 128).T, np.float32)
    ii, jj = np.indices((128, 128))
    mask_inf = np.where(jj > ii, np.float32(-1e30), np.float32(0.0))
    mask_low = (jj <= ii).astype(np.float32).T
    return {
        "win": win, "wdt": wdt, "dtb": dtb, "alog": alog, "dvec": dvec,
        "convw": convw, "convb": convb, "wout": wout, "ident": ident,
        "bd1": np.ascontiguousarray(bd1.reshape(4096, 128)), "bd2": np.ascontiguousarray(bd2.reshape(4096, 512)), "wcT": wcT,
        "wqT": wqT, "wkT": wkT, "wvT": wvT, "woT": woT, "qkvT": qkvT,
        "projT": projT, "routT": routT, "w1": w1, "w2": w2,
        "ln1w": ln1w, "ln1b": ln1b, "ln2w": ln2w, "ln2b": ln2b,
        "mask_inf": mask_inf, "mask_low": mask_low,
    }


def build_program():
    nc = bass.Bass("TRN2", target_bir_lowering=False, detect_race_conditions=False)

    def ckpt(name):
        if STOP_AFTER == name:
            raise _StopEmission()

    def din(name, shape):
        return nc.dram_tensor(name, list(shape), F32, kind="ExternalInput").ap()

    x0T = din("x0T", (D, T))
    win = din("win", (D, WIN_COLS))
    wdt = din("wdt", (DTR, 2048))
    dtb = din("dtb", (128, 16))
    alog = din("alog", (128, 256))
    dvec = din("dvec", (128, 16))
    convw = din("convw", (128, 16))
    convb = din("convb", (128, 4))
    wout = din("wout", (2048, D))
    ident_in = din("ident", (128, 128))
    bd1_in = din("bd1", (4 * 8 * 128, 128))
    bd2_in = din("bd2", (4 * 8 * 128, 512))
    wcT_in = din("wcT", (16384, D))
    wqT_in = din("wqT", (D, D))
    wkT_in = din("wkT", (D, 256))
    wvT_in = din("wvT", (D, 256))
    woT_in = din("woT", (D, D))
    qkvT_in = din("qkvT", (D, 1536))
    projT_in = din("projT", (D, D))
    routT_in = din("routT", (D, 8))
    w1_in = din("w1", (8 * 1024, 2048))
    w2_in = din("w2", (8 * 2048, 1024))
    ln1w_in = din("ln1w", (128, 8))
    ln1b_in = din("ln1b", (128, 8))
    ln2w_in = din("ln2w", (128, 8))
    ln2b_in = din("ln2b", (128, 8))
    mski_in = din("mask_inf", (128, 128))
    tmask_in = din("tmask", (128, 8))
    mskl_in = din("mask_low", (128, 128))

    out_hbm = nc.dram_tensor("out", [T, D], F32, kind="ExternalOutput").ap()
    dbg_out = nc.dram_tensor("dbg_out", [128, 5264], F32, kind="ExternalOutput").ap() if TAP_P5G else None

    win16 = nc.dram_tensor("win16_scr", [D, WIN_COLS], BF16).ap()
    wout16 = nc.dram_tensor("wout16_scr", [2048, D], BF16).ap()
    bd1_16 = nc.dram_tensor("bd1_scr", [4 * 8 * 128, 128], BF16).ap()
    bd2_16 = nc.dram_tensor("bd2_scr", [4 * 8 * 128, 512], BF16).ap()
    wc16 = nc.dram_tensor("wc16_scr", [16384, D], BF16).ap()
    wq16d = nc.dram_tensor("wq16_scr", [D, D], BF16).ap()
    wk16d = nc.dram_tensor("wk16_scr", [D, 256], BF16).ap()
    wv16d = nc.dram_tensor("wv16_scr", [D, 256], BF16).ap()
    wo16d = nc.dram_tensor("wo16_scr", [D, D], BF16).ap()
    qkv16d = nc.dram_tensor("qkv16_scr", [D, 1536], BF16).ap()
    proj16d = nc.dram_tensor("proj16_scr", [D, D], BF16).ap()
    rout16d = nc.dram_tensor("rout16_scr", [D, 8], BF16).ap()
    w116d = nc.dram_tensor("w116_scr", [8 * 1024, 2048], BF16).ap()
    w216d = nc.dram_tensor("w216_scr", [8 * 2048, 1024], BF16).ap()
    crow_scr = nc.dram_tensor("crow_scr", [2, T], BF16).ap()
    xn1_scr = nc.dram_tensor("xn1_scr", [D, T], BF16).ap()
    ln1_scr = nc.dram_tensor("ln1_scr", [D, T], BF16).ap()
    ln2_scr = nc.dram_tensor("ln2_scr", [D, T], BF16).ap()
    qf_scr = nc.dram_tensor("qf_scr", [D, T], BF16).ap()
    kf_scr = nc.dram_tensor("kf_scr", [256, T], BF16).ap()
    at_scr = nc.dram_tensor("at_scr", [T, D], BF16).ap()
    af_scr = nc.dram_tensor("af_scr", [D, T], BF16).ap()
    h2_scr = nc.dram_tensor("h2_scr", [D, T], F32).ap()
    kvloc = nc.dram_tensor("kvloc", [T, 512], BF16).ap()
    kvg = nc.dram_tensor("kvg", [8 * T, 512], BF16, addr_space="Shared").ap()
    xnb_scr = nc.dram_tensor("xnb_scr", [D, T], BF16).ap()
    row_scr = nc.dram_tensor("row_scr", [8, T], F32).ap()

    ctx = {}
    def sb(name, shape, dtype, side="left"):
        cm = nc.sbuf_tensor(name, list(shape), dtype, side=side)
        t = cm.__enter__(); ctx[name] = cm; return t

    def sb_free(*names):
        for n in names:
            ctx.pop(n).__exit__(None, None, None)

    acc = [sb(f"acc{i}", [128, T], F32) for i in range(KD)]
    negA = sb("negA", [128, 256], F32)
    dtb2 = sb("dtb2", [128, 16], F32)
    dvec_s = sb("dvec_s", [128, 16], F32)
    convw_s = sb("convw_s", [128, 16], F32)
    convb_s = sb("convb_s", [128, 4], F32)
    wdt16 = sb("wdt16", [64, 2048], BF16)
    ones_col = sb("ones_col", [128, 1], F32)
    ident = sb("ident_sb", [128, 128], F32)
    ident16 = sb("ident16_sb", [128, 128], BF16)
    eps_c = sb("eps_c", [1, 1], F32)
    h_carry = sb("h_carry", [128, 256], F32)


    WBL = 256
    win_blk = [sb(f"win_blk{i}", [128, 8 * WBL], BF16) for i in range(2)]
    xnb_ch = sb("xnb_ch", [128, KD * TC], BF16)
    siluz = [sb(f"siluz{i}", [128, TC], BF16) for i in range(PP)]
    y2b = siluz
    xpre = [sb(f"xpre{i}", [128, TC + 4], F32) for i in range(4)]
    xcs = [sb(f"xcs{i}", [128, TC], F32) for i in range(4)]
    braw16 = sb("braw16", [128, 4 * TC], BF16)
    brep = [sb(f"brep{i}", [128, TC], BF16) for i in range(NN)]
    cfull = [sb(f"cfull{i}", [128, TC], BF16) for i in range(NN)]
    dtr16 = sb("dtr16", [64, TC], BF16)

    bc_a = sb("bc_a", [128, T], F32, side="right")
    bc_b = sb("bc_b", [128, T], F32, side="right")
    sq_t = [sb(f"sq_t{i}", [128, 512], F32, side="right") for i in range(2)]
    sum_r = sb("sum_r", [1, T], F32, side="right")
    sq_r = sb("sq_r", [1, T], F32, side="right")

    pctx = {}
    def psum(name, shape, side):
        cm = nc.psum_tensor(name, list(shape), F32, side=side)
        t = cm.__enter__(); pctx[name] = cm; return t
    def psum_free(*names):
        for n in names:
            pctx.pop(n).__exit__(None, None, None)
    pb = [psum(f"pb{i}", [128, 512], "left" if i < 4 else "right")
          for i in range(8)]

    sems = {}
    for s in ("s_c", "s_x", "s_ph2", "s_aux", "s_nb", "s_wa", "s_wb", "s_sa", "s_sb",
              "s_repb", "s_r0", "s_r1", "s_q0", "s_q1", "s_o0", "s_o1", "s_cast", "s_cc",
              "s_wca", "s_wcb", "s_ba", "s_bb", "s_t", "s_v", "s_s"):
        sems[s] = nc.semaphore(s).__enter__()

    ops = {e: [] for e in ("sync", "scalar", "vector", "tensor", "gpsimd")}
    cnt = {}

    def _inc(name, by=1):
        cnt[name] = cnt.get(name, 0) + by
        return cnt[name]

    def cur(name):
        return cnt.get(name, 0)

    def emit(eng, fn):
        ops[eng].append(fn)

    def wait(eng, sem_name, val):
        if val <= 0:
            return
        emit(eng, lambda e, s=sems[sem_name], v=val: e.wait_ge(s, v))

    def t_mm(out_ap, lhsT, rhs, start, stop):
        tgt = _inc("s_t")
        emit("tensor", lambda e, o=out_ap, l=lhsT, r=rhs, st=start, sp=stop:
             e.matmul(o, l, r, start=st, stop=sp).then_inc(sems["s_t"], 1))
        return tgt

    def t_tr(out_ap, in_ap, bf=False):
        tgt = _inc("s_t")
        idt = ident16 if bf else ident
        emit("tensor", lambda e, o=out_ap, i=in_ap, idt=idt:
             e.transpose(o, i, idt[:]).then_inc(sems["s_t"], 1))
        return tgt

    def v_op(fn):
        tgt = _inc("s_v")
        emit("vector", lambda e, fn=fn: fn(e).then_inc(sems["s_v"], 1))
        return tgt

    def s_op(fn):
        tgt = _inc("s_s")
        emit("scalar", lambda e, fn=fn: fn(e).then_inc(sems["s_s"], 1))
        return tgt

    def dma(eng, sem_name, out_ap, in_ap, slow=False):
        tgt = _inc(sem_name, 16)
        if slow:
            emit(eng, lambda e, o=out_ap, i=in_ap, s=sems[sem_name]:
                 e.dma_start(out=o, in_=i, allow_slow_non_contiguous=True).then_inc(s, 16))
        else:
            emit(eng, lambda e, o=out_ap, i=in_ap, s=sems[sem_name]:
                 e.dma_start(out=o, in_=i).then_inc(s, 16))
        return tgt

    def vsync():
        wait("vector", "s_v", cur("s_v"))

    def barrier():
        snap = [(n, cur(n)) for n in ("s_t", "s_v", "s_s", "s_c", "s_x", "s_ph2",
                                      "s_aux", "s_nb", "s_wa", "s_wb", "s_sa", "s_sb",
                                      "s_repb", "s_r0", "s_r1", "s_q0", "s_q1", "s_cast",
                                      "s_wca", "s_wcb", "s_ba", "s_bb")]
        own = {"tensor": "s_t", "vector": "s_v", "scalar": "s_s"}
        for eng in ("sync", "scalar", "vector", "tensor", "gpsimd"):
            for n, v in snap:
                if v > 0 and own.get(eng) != n:
                    wait(eng, n, v)

    try:
        dma("sync", "s_c", negA[:], alog)
        dma("sync", "s_c", dtb2[:], dtb)
        dma("sync", "s_c", dvec_s[:], dvec)
        dma("sync", "s_c", convw_s[:], convw)
        dma("sync", "s_c", convb_s[:], convb)
        dma("sync", "s_c", bc_a[0:64, 0:2048], wdt)
        dma("sync", "s_c", ident[:], ident_in)
        ld0 = cur("s_c")

        wait("vector", "s_c", ld0)
        v_op(lambda e: e.memset(ones_col[:], 1.0))
        v_op(lambda e: e.memset(eps_c[:], EPS))
        v_op(lambda e: e.tensor_copy(ident16[:], ident[:]))
        v_op(lambda e: e.memset(h_carry[:], 0.0))
        v_op(lambda e: e.tensor_copy(wdt16[:], bc_a[0:64, 0:2048]))
        v_op(lambda e: e.tensor_scalar(dtb2[:], dtb2[:], 2.0, None, op0=ALU.mult))
        for i in range(4):
            v_op(lambda e, i=i: e.memset(xpre[i][:, 0:4], 0.0))
        wait("scalar", "s_c", ld0)
        s_op(lambda e: e.activation(negA[:], negA[:], AF.Exp))
        wait("vector", "s_s", cur("s_s"))
        v_op(lambda e: e.tensor_scalar(negA[:], negA[:], -1.0, None, op0=ALU.mult))

        for k in range(KD):
            dma("sync", "s_x", acc[k][:], x0T[k * 128:(k + 1) * 128, :])
        ld_x = cur("s_x")

        wait("tensor", "s_x", ld_x)
        wait("tensor", "s_v", cur("s_v"))
        wait("scalar", "s_x", ld_x)
        sq_mm = {}
        for sl in range(4):
            c0 = sl * 512
            for k in range(KD):
                t_mm(pb[0][0:1, 0:512], ones_col[:], acc[k][:, c0:c0 + 512],
                     start=(k == 0), stop=(k == KD - 1))
            for k in range(KD):
                m = sl * KD + k
                if m - 2 in sq_mm:
                    wait("scalar", "s_t", sq_mm[m - 2])
                sq = s_op(lambda e, k=k, c0=c0: e.activation(
                    sq_t[k % 2][:], acc[k][:, c0:c0 + 512], AF.Square))
                wait("tensor", "s_s", sq)
                sq_mm[m] = t_mm(pb[1][0:1, 0:512], ones_col[:], sq_t[k % 2][:],
                                start=(k == 0), stop=(k == KD - 1))
            mmd = cur("s_t")
            wait("vector", "s_t", mmd)
            v_op(lambda e, c0=c0, b0=pb[0]: e.tensor_copy(sum_r[0:1, c0:c0 + 512], b0[0:1, 0:512]))
            v_op(lambda e, c0=c0, b1=pb[1]: e.tensor_copy(sq_r[0:1, c0:c0 + 512], b1[0:1, 0:512]))
            wait("tensor", "s_v", cur("s_v"))

        v_op(lambda e: e.tensor_scalar(sum_r[:], sum_r[:], 1.0 / D, None, op0=ALU.mult))
        v_op(lambda e: e.tensor_scalar(sq_r[:], sq_r[:], 1.0 / D, None, op0=ALU.mult))
        v_op(lambda e: e.tensor_tensor(bc_a[0:1, :], sum_r[:], sum_r[:], op=ALU.mult))
        v_op(lambda e: e.tensor_tensor(sq_r[:], sq_r[:], bc_a[0:1, :], op=ALU.subtract))
        wait("scalar", "s_v", cur("s_v"))
        rs = s_op(lambda e: e.activation(sq_r[:], sq_r[:], AF.Sqrt, bias=eps_c[0:1, 0:1]))
        wait("vector", "s_s", rs)
        v_op(lambda e: e.reciprocal(sq_r[:], sq_r[:]))
        v_op(lambda e: e.tensor_tensor(sum_r[:], sum_r[:], sq_r[:], op=ALU.mult))
        v_op(lambda e: e.tensor_scalar(sum_r[:], sum_r[:], -1.0, None, op0=ALU.mult))
        rows_done = cur("s_v")

        wait("scalar", "s_v", rows_done)
        dma("scalar", "s_aux", row_scr[4:5, :], sq_r[:])
        dma("scalar", "s_aux", row_scr[5:6, :], sum_r[:])
        wait("scalar", "s_aux", cur("s_aux"))
        dma("scalar", "s_aux", bc_a[:], row_scr[4:5, :].to_broadcast([128, T]))
        dma("scalar", "s_aux", bc_b[:], row_scr[5:6, :].to_broadcast([128, T]))
        bc_done = cur("s_aux")

        wait("vector", "s_aux", bc_done)
        for k in range(KD):
            for sl in range(4):
                c0 = sl * 512
                v_op(lambda e, k=k, c0=c0, b=sl % 2: e.tensor_tensor(
                    xcs[b][:], acc[k][:, c0:c0 + 512], bc_a[:, c0:c0 + 512], op=ALU.mult))
                v_op(lambda e, k=k, c0=c0, b=sl % 2: e.tensor_tensor(
                    braw16[:, c0:c0 + 512], xcs[b][:], bc_b[:, c0:c0 + 512], op=ALU.add))
            cst = cur("s_v")
            wait("scalar", "s_v", cst)
            dma("scalar", "s_aux", xnb_scr[k * 128:(k + 1) * 128, :], braw16[:, 0:T])
            wait("vector", "s_aux", cur("s_aux"))

        dma("gpsimd", "s_cast", win16, win)
        dma("gpsimd", "s_cast", wout16, wout)
        dma("gpsimd", "s_cast", bd1_16, bd1_in)
        dma("gpsimd", "s_cast", bd2_16, bd2_in)
        for q in range(4):
            dma("gpsimd", "s_cast", wc16[q * 4096:(q + 1) * 4096, :],
                wcT_in[q * 4096:(q + 1) * 4096, :])
        dma("gpsimd", "s_cast", wq16d, wqT_in)
        dma("gpsimd", "s_cast", wk16d, wkT_in)
        dma("gpsimd", "s_cast", wv16d, wvT_in)
        dma("gpsimd", "s_cast", wo16d, woT_in)
        dma("gpsimd", "s_cast", qkv16d, qkvT_in)
        dma("gpsimd", "s_cast", proj16d, projT_in)
        dma("gpsimd", "s_cast", rout16d, routT_in)
        for q in range(4):
            dma("gpsimd", "s_cast", w116d[q * 2048:(q + 1) * 2048, :],
                w1_in[q * 2048:(q + 1) * 2048, :])
            dma("gpsimd", "s_cast", w216d[q * 4096:(q + 1) * 4096, :],
                w2_in[q * 4096:(q + 1) * 4096, :])

        barrier()
        sb_free("sq_r", "sum_r", "sq_t1", "sq_t0", "bc_b", "bc_a")

        xrep_t = [sb(f"xrep{i}", [128, TC], F32, side="right") for i in range(2)]
        sp_x = sb("sp_x", [128, TC], F32, side="right")
        sp_r = sb("sp_r", [128, TC], F32, side="right")
        sp_t = sb("sp_t", [128, TC], F32, side="right")
        delta_t = sb("delta_t", [128, TC], F32, side="right")
        dx_t = sb("dx_t", [128, TC], F32, side="right")
        a_t = [sb(f"a_t{i}", [128, TC], F32, side="right") for i in range(2)]
        bb_t = [sb(f"bb_t{i}", [128, TC], F32, side="right") for i in range(2)]
        h_t = [sb(f"h_t{i}", [128, TC], F32, side="right") for i in range(2)]
        prod_t = sb("prod_t", [128, TC], F32, side="right")
        y_p = sb("y_p", [128, TC], F32, side="right")
        wsl = [sb(f"wsl{i}", [128, D], BF16, side="right") for i in range(2)]

        ckpt('ln0')
        for c in range(NCH):
            t0 = c * TC
            for k in range(KD):
                dma("sync", "s_nb", xnb_ch[:, k * TC:(k + 1) * TC],
                    xnb_scr[k * 128:(k + 1) * 128, t0:t0 + TC])
            ld_x0 = cur("s_nb")

            n_blocks = 21
            evac_tgts = []
            blk_mm_done = []
            for j in range(n_blocks):
                bw = WBL if j < 20 else 64
                buf = j % 2
                if j >= 2:
                    wait("sync", "s_t", blk_mm_done[j - 2])
                wsem = "s_wa" if buf == 0 else "s_wb"
                for k in range(KD):
                    dma("sync", wsem, win_blk[buf][:, k * WBL:k * WBL + bw],
                        win16[k * 128:(k + 1) * 128, j * WBL:j * WBL + bw])
                wait("tensor", wsem, cur(wsem))
                wait("tensor", "s_nb", ld_x0)
                n_tiles = 2 if j < 20 else 1
                for oti in range(n_tiles):
                    ot = 2 * j + oti
                    w_ = 128 if j < 20 else 64
                    bank = pb[ot % 2]
                    if len(evac_tgts) >= 2:
                        wait("tensor", "s_s", evac_tgts[-2])
                    for k in range(KD):
                        t_mm(bank[0:w_, 0:TC],
                             win_blk[buf][:, k * WBL + oti * 128:k * WBL + oti * 128 + w_],
                             xnb_ch[:, k * TC:(k + 1) * TC],
                             start=(k == 0), stop=(k == KD - 1))
                    mmd = cur("s_t")
                    wait("scalar", "s_t", mmd)
                    ps = bank[0:w_, 0:TC]
                    if ot < 16:
                        tgt = s_op(lambda e, i=ot, ps=ps: e.activation(siluz[i][:], ps, AF.Silu))
                    elif ot < 20:
                        tgt = s_op(lambda e, i=ot - 16, ps=ps: e.copy(xpre[i][:, 4:4 + TC], ps))
                    elif ot < 24:
                        tgt = s_op(lambda e, i=ot - 20, ps=ps: e.copy(
                            braw16[:, i * TC:(i + 1) * TC], ps))
                    elif ot < 40:
                        tgt = s_op(lambda e, i=ot - 24, ps=ps: e.copy(cfull[i][:], ps))
                    else:
                        tgt = s_op(lambda e, ps=ps: e.copy(dtr16[:], ps))
                    evac_tgts.append(tgt)
                blk_mm_done.append(cur("s_t"))
            evac_all = cur("s_s")

            wait("gpsimd", "s_s", evac_all)
            brep_tgts = []
            for n in range(NN):
                src = braw16[32 * (n % 4):32 * (n % 4) + 32, (n // 4) * TC:(n // 4 + 1) * TC]
                brep_tgts.append(dma("gpsimd", "s_repb",
                                     brep[n][:], src.unsqueeze(1).to_broadcast([32, 4, TC])))

            wait("vector", "s_s", evac_all)
            for i in range(4):
                v_op(lambda e, i=i: e.tensor_scalar(
                    xcs[i][:], xpre[i][:, 1:1 + TC], convw_s[:, 4 * i:4 * i + 1], None, op0=ALU.mult))
                for kk in range(1, 4):
                    v_op(lambda e, i=i, kk=kk: e.scalar_tensor_tensor(
                        xcs[i][:], xpre[i][:, 1 + kk:1 + kk + TC],
                        convw_s[:, 4 * i + kk:4 * i + kk + 1], xcs[i][:],
                        op0=ALU.mult, op1=ALU.add))
            convv = cur("s_v")
            wait("scalar", "s_v", convv)
            for i in range(4):
                s_op(lambda e, i=i: e.activation(xcs[i][:], xcs[i][:], AF.Silu,
                                                 bias=convb_s[:, i:i + 1]))
            conv_done = cur("s_s")
            wait("vector", "s_s", conv_done)
            for i in range(4):
                v_op(lambda e, i=i: e.tensor_copy(xpre[i][:, 1:4], xpre[i][:, 1 + TC:4 + TC]))

            xrep_consumer = {}
            prev_delta_v = 0
            for pi in range(PP):
                pbuf = pi % 2
                wait("gpsimd", "s_s", conv_done)
                if pi >= 2:
                    wait("gpsimd", "s_v", xrep_consumer[pi - 2])
                xr = dma("gpsimd", "s_r0" if pbuf == 0 else "s_r1", xrep_t[pbuf][:],
                         xcs[pi // 4][32 * (pi % 4):32 * (pi % 4) + 32, :]
                         .unsqueeze(1).to_broadcast([32, 4, TC]))
                bank = pb[2 + pbuf]
                wait("tensor", "s_s", cur("s_s"))
                t_mm(bank[:, 0:TC], wdt16[:, pi * 128:(pi + 1) * 128], dtr16[:],
                     start=True, stop=True)
                dmm = cur("s_t")
                wait("scalar", "s_t", dmm)
                wait("scalar", "s_v", prev_delta_v)
                s_op(lambda e, bank=bank: e.copy(sp_x[:], bank[:, 0:TC]))
                s_op(lambda e, pi=pi: e.activation(sp_r[:], sp_x[:], AF.Relu,
                                                   bias=dtb2[:, pi:pi + 1]))
                wait("vector", "s_s", cur("s_s"))
                v_op(lambda e: e.scalar_tensor_tensor(
                    sp_t[:], sp_r[:], -2.0, sp_x[:], op0=ALU.mult, op1=ALU.add))
                wait("scalar", "s_v", cur("s_v"))
                s_op(lambda e, pi=pi: e.activation(sp_t[:], sp_t[:], AF.Exp,
                                                   bias=dtb2[:, pi:pi + 1]))
                s_op(lambda e: e.activation(sp_t[:], sp_t[:], AF.Ln, bias=ones_col[:, 0:1]))
                wait("vector", "s_s", cur("s_s"))
                v_op(lambda e: e.tensor_tensor(delta_t[:], sp_r[:], sp_t[:], op=ALU.add))

                prev_delta_v = cur("s_v")
                wait("vector", "s_r0" if pbuf == 0 else "s_r1", xr)

                v_op(lambda e, pbuf=pbuf: e.tensor_tensor(
                    dx_t[:], delta_t[:], xrep_t[pbuf][:], op=ALU.mult))

                if pi == 0:
                    wait("vector", "s_repb", brep_tgts[-1])
                for n in range(NN):
                    nb = n % 2
                    idx = pi * 16 + n
                    av = v_op(lambda e, nb=nb, idx=idx: e.tensor_scalar(
                        a_t[nb][:], delta_t[:], negA[:, idx:idx + 1], None, op0=ALU.mult))
                    wait("scalar", "s_v", av)
                    ae = s_op(lambda e, nb=nb: e.activation(a_t[nb][:], a_t[nb][:], AF.Exp))
                    v_op(lambda e, nb=nb, n=n: e.tensor_tensor(
                        bb_t[nb][:], dx_t[:], brep[n][:], op=ALU.mult))
                    wait("vector", "s_s", ae)

                    sc_t = v_op(lambda e, nb=nb, idx=idx: e.tensor_tensor_scan(
                        h_t[nb][:], a_t[nb][:], bb_t[nb][:], h_carry[:, idx:idx + 1],
                        op0=ALU.mult, op1=ALU.add))
                    wait("vector", "s_v", sc_t)

                    v_op(lambda e, nb=nb, idx=idx: e.tensor_copy(
                        h_carry[:, idx:idx + 1], h_t[nb][:, TC - 1:TC]))
                    if n == 0:
                        v_op(lambda e, nb=nb, n=n: e.tensor_tensor(
                            y_p[:], h_t[nb][:], cfull[n][:], op=ALU.mult))
                    else:
                        v_op(lambda e, nb=nb, n=n: e.tensor_tensor(
                            prod_t[:], h_t[nb][:], cfull[n][:], op=ALU.mult))
                        v_op(lambda e: e.tensor_tensor(y_p[:], y_p[:], prod_t[:], op=ALU.add))
                v_op(lambda e, pbuf=pbuf, pi=pi: e.scalar_tensor_tensor(
                    prod_t[:], xrep_t[pbuf][:], dvec_s[:, pi:pi + 1], y_p[:],
                    op0=ALU.mult, op1=ALU.add))
                xrep_consumer[pi] = cur("s_v")
                v_op(lambda e, pi=pi: e.tensor_tensor(
                    y2b[pi][:], prod_t[:], siluz[pi][:], op=ALU.mult))
            scan_done = cur("s_v")

            t_base = cur("s_t")
            slab_tgts = []
            for pi in range(PP):
                if pi >= 2:
                    wait("sync", "s_t", t_base + 8 * (pi - 1))
                slab_tgts.append((("s_sa" if pi % 2 == 0 else "s_sb"),
                                  dma("sync", "s_sa" if pi % 2 == 0 else "s_sb",
                                      wsl[pi % 2][:], wout16[pi * 128:(pi + 1) * 128, :])))

            wait("tensor", "s_v", scan_done)
            wait("tensor", "s_s", cur("s_s"))
            for pi in range(PP):
                wait("tensor", slab_tgts[pi][0], slab_tgts[pi][1])
                for o in range(8):
                    t_mm(pb[o][:, 0:TC], wsl[pi % 2][:, o * 128:(o + 1) * 128], y2b[pi][:],
                         start=(pi == 0), stop=(pi == PP - 1))
            wout_done = cur("s_t")
            wait("vector", "s_t", wout_done)
            for o in range(8):
                v_op(lambda e, o=o, t0=t0, bk=pb[o]: e.scalar_tensor_tensor(
                    acc[o][:, t0:t0 + TC], bk[:, 0:TC], 1.0, acc[o][:, t0:t0 + TC],
                    op0=ALU.mult, op1=ALU.add))
            barrier()

        ckpt('mamba')
        sb_free("wsl1", "wsl0", "y_p", "prod_t", "h_t1", "h_t0", "bb_t1", "bb_t0",
                "a_t1", "a_t0", "dx_t", "delta_t", "sp_t", "sp_r", "sp_x",
                "xrep1", "xrep0")
        sb_free("dtr16", *[f"cfull{i}" for i in reversed(range(NN))],
                *[f"brep{i}" for i in reversed(range(NN))], "braw16",
                *[f"xcs{i}" for i in reversed(range(4))],
                *[f"xpre{i}" for i in reversed(range(4))],
                *[f"siluz{i}" for i in reversed(range(PP))], "xnb_ch",
                "win_blk1", "win_blk0")

        TF = 256
        xnb_t = sb("xnb_t", [128, KD * TF], BF16)
        y_t = sb("y_t", [128, 32 * TF], BF16)
        h_half = sb("h_half", [128, 64 * TF], BF16)
        wc_sb = [sb(f"wc_sb{i}", [128, 8 * 1024], BF16) for i in range(2)]
        bd1_sb = sb("bd1_sb", [128, 32 * 128], BF16)
        bd2_sb = [sb(f"bd2_sb{i}", [128, 512], BF16) for i in range(2)]

        bd1_ld = dma("sync", "s_c", bd1_sb[:],
                     bd1_16.rearrange("(a p) c -> p a c", p=128))
        wait("tensor", "s_c", bd1_ld)

        s1_mm_end = 0
        s2_mm_end = 0
        for it in range(8):
            c0 = it * TF
            if it >= 1:
                wait("sync", "s_t", s1_mm_end)
            for k in range(KD):
                dma("sync", "s_nb", xnb_t[:, k * TF:(k + 1) * TF],
                    xnb_scr[k * 128:(k + 1) * 128, c0:c0 + TF])
            ld_t = cur("s_nb")
            wait("tensor", "s_nb", ld_t)
            wait("scalar", "s_t", s2_mm_end)
            ev_prev = []
            for blk in range(32):
                h, g8 = divmod(blk, 8)
                bank = pb[blk % 2]
                if len(ev_prev) >= 2:
                    wait("tensor", "s_s", ev_prev[-2])
                if blk < 2:
                    wait("tensor", "s_v", cur("s_v"))
                t_mm(bank[:, 0:TF], bd1_sb[:, blk * 128:(blk + 1) * 128],
                     xnb_t[:, g8 * TF:(g8 + 1) * TF], start=True, stop=True)
                mmd = cur("s_t")
                wait("scalar", "s_t", mmd)
                ev_prev.append(s_op(lambda e, blk=blk, bank=bank: e.copy(
                    y_t[:, blk * TF:(blk + 1) * TF], bank[:, 0:TF])))
            s1_mm_end = cur("s_t")
            s1_done = cur("s_s")

            for half in range(2):
                wait("tensor", "s_s", s1_done)
                wait("tensor", "s_v", cur("s_v"))
                s2_base = cur("s_t")
                gel_tgts = []
                for bi in range(16):
                    blk = half * 16 + bi
                    bsem = "s_ba" if bi % 2 == 0 else "s_bb"
                    if bi >= 2:
                        wait("sync", "s_t", s2_base + 4 * (bi - 1))
                    dma("sync", bsem, bd2_sb[bi % 2][:],
                        bd2_16[blk * 128:(blk + 1) * 128, :])
                    wait("tensor", bsem, cur(bsem))
                    bankA = pb[(bi % 2) * 2]
                    bankB = pb[(bi % 2) * 2 + 1]
                    if len(gel_tgts) >= 4:
                        wait("tensor", "s_s", gel_tgts[-3])
                    for sl4 in range(4):
                        bank = bankA if sl4 < 2 else bankB
                        off = (sl4 % 2) * TF
                        t_mm(bank[:, off:off + TF], bd2_sb[bi % 2][:, sl4 * 128:(sl4 + 1) * 128],
                             y_t[:, blk * TF:(blk + 1) * TF], start=True, stop=True)
                    mmd = cur("s_t")
                    wait("scalar", "s_t", mmd)
                    fl = bi * 4
                    gel_tgts.append(s_op(lambda e, bankA=bankA, fl=fl: e.activation(
                        h_half[:, fl * TF:(fl + 2) * TF], bankA[:, 0:2 * TF], AF.Gelu)))
                    gel_tgts.append(s_op(lambda e, bankB=bankB, fl=fl: e.activation(
                        h_half[:, (fl + 2) * TF:(fl + 4) * TF], bankB[:, 0:2 * TF], AF.Gelu)))
                s2_mm_end = cur("s_t")
                s2_done = cur("s_s")

                wait("tensor", "s_s", s2_done)
                wait("tensor", "s_v", cur("s_v"))
                cp_base = cur("s_t")
                for fg in range(8):
                    wcsem = "s_wca" if fg % 2 == 0 else "s_wcb"
                    if fg >= 2:
                        wait("sync", "s_t", cp_base + 64 * (fg - 1))
                    r0 = (half * 64 + fg * 8) * 128
                    dma("sync", wcsem, wc_sb[fg % 2][:],
                        wc16[r0:r0 + 1024, :].rearrange("(a p) c -> p a c", p=128))
                    wait("tensor", wcsem, cur(wcsem))
                    for fc8 in range(8):
                        fcl = fg * 8 + fc8
                        for o in range(8):
                            t_mm(pb[o][:, 0:TF],
                                 wc_sb[fg % 2][:, fc8 * 1024 + o * 128:fc8 * 1024 + o * 128 + 128],
                                 h_half[:, fcl * TF:(fcl + 1) * TF],
                                 start=(fcl == 0), stop=(fcl == 63))
                cp_done = cur("s_t")
                wait("vector", "s_t", cp_done)
                for o in range(8):
                    v_op(lambda e, o=o, c0=c0, bk=pb[o]: e.scalar_tensor_tensor(
                        acc[o][:, c0:c0 + TF], bk[:, 0:TF], 1.0, acc[o][:, c0:c0 + TF],
                        op0=ALU.mult, op1=ALU.add))
        barrier()

        ckpt('l0ffn')
        sb_free("bd2_sb1", "bd2_sb0", "bd1_sb", "wc_sb1", "wc_sb0", "h_half",
                "y_t", "xnb_t")

        wg = sb("wg", [128, 16 * 8], F32)

        bc1_a = sb("bc1_a", [128, T], F32)
        bc1_b = sb("bc1_b", [128, T], F32)
        sum1_r = sb("sum1_r", [1, T], F32)
        sq1_r = sb("sq1_r", [1, T], F32)
        xtmp = sb("xtmp", [128, 512], F32)
        sqt = [sb(f"sqt{i}", [128, 512], F32) for i in range(2)]
        ln1w_s = sb("ln1w_s", [128, 8], F32)
        ln1b_s = sb("ln1b_s", [128, 8], F32)
        ln2w_s = sb("ln2w_s", [128, 8], F32)
        ln2b_s = sb("ln2b_s", [128, 8], F32)
        mski = sb("mski", [128, 128], F32)
        mskl = sb("mskl", [128, 128], F32)
        tmask = sb("tmask_sb", [128, 8], F32)
        stg16 = [sb(f"stg16_{i}", [128, T], BF16) for i in range(2)]
        stg32 = [sb(f"stg32_{i}", [128, 512], F32) for i in range(2)]

        dma("sync", "s_c", ln1w_s[:], ln1w_in)
        dma("sync", "s_c", ln1b_s[:], ln1b_in)
        dma("sync", "s_c", ln2w_s[:], ln2w_in)
        dma("sync", "s_c", ln2b_s[:], ln2b_in)
        dma("sync", "s_c", mski[:], mski_in)
        dma("sync", "s_c", mskl[:], mskl_in)
        dma("sync", "s_c", tmask[:], tmask_in)
        ln_c = cur("s_c")
        wait("vector", "s_c", ln_c)
        wait("scalar", "s_c", ln_c)

        class Streamer:
            def __init__(self, name, shape, dtype, sems, cons_sem="s_t", n_buf=2):
                self.n_buf = n_buf
                self.tiles = [sb(f"{name}{i}", list(shape), dtype) for i in range(n_buf)]
                self.name = name
                self.sems = sems
                self.cons_sem = cons_sem
                self.i = 0
                self.cons = [0] * n_buf

            def load(self, src_ap):
                b = self.i % self.n_buf
                if self.cons[b]:
                    wait("sync", self.cons_sem, self.cons[b])
                tgt = dma("sync", self.sems[b], self.tiles[b][:], src_ap)
                self.i += 1
                return b, self.sems[b], tgt

            def consumed(self, b):
                self.cons[b] = cur(self.cons_sem)

            def free(self):
                sb_free(*[f"{self.name}{i}" for i in reversed(range(self.n_buf))])

        def _ln_rows_tail():
            v_op(lambda e: e.tensor_scalar(sum1_r[:], sum1_r[:], 1.0 / D, None, op0=ALU.mult))
            v_op(lambda e: e.tensor_scalar(sq1_r[:], sq1_r[:], 1.0 / D, None, op0=ALU.mult))
            v_op(lambda e: e.tensor_tensor(bc1_a[0:1, :], sum1_r[:], sum1_r[:], op=ALU.mult))
            v_op(lambda e: e.tensor_tensor(sq1_r[:], sq1_r[:], bc1_a[0:1, :], op=ALU.subtract))
            wait("scalar", "s_v", cur("s_v"))
            rs2 = s_op(lambda e: e.activation(sq1_r[:], sq1_r[:], AF.Sqrt, bias=eps_c[0:1, 0:1]))
            wait("vector", "s_s", rs2)
            v_op(lambda e: e.reciprocal(sq1_r[:], sq1_r[:]))
            v_op(lambda e: e.tensor_tensor(sum1_r[:], sum1_r[:], sq1_r[:], op=ALU.mult))
            v_op(lambda e: e.tensor_scalar(sum1_r[:], sum1_r[:], -1.0, None, op0=ALU.mult))
            rows_d = cur("s_v")
            wait("scalar", "s_v", rows_d)
            dma("scalar", "s_aux", row_scr[4:5, :], sq1_r[:])
            dma("scalar", "s_aux", row_scr[5:6, :], sum1_r[:])
            wait("scalar", "s_aux", cur("s_aux"))
            dma("scalar", "s_aux", bc1_a[:], row_scr[4:5, :].to_broadcast([128, T]))
            dma("scalar", "s_aux", bc1_b[:], row_scr[5:6, :].to_broadcast([128, T]))
            return cur("s_aux")

        def emit_ln_stats_sb(src):
            wait("tensor", "s_v", cur("s_v"))
            wait("tensor", "s_s", cur("s_s"))
            sqm2 = {}
            for sl in range(4):
                c0 = sl * 512
                for k in range(KD):
                    t_mm(pb[0][0:1, 0:512], ones_col[:], src[k][:, c0:c0 + 512],
                         start=(k == 0), stop=(k == KD - 1))
                for k in range(KD):
                    m = sl * KD + k
                    if m - 2 in sqm2:
                        wait("scalar", "s_t", sqm2[m - 2])
                    sq = s_op(lambda e, k=k, c0=c0: e.activation(
                        sqt[k % 2][:], src[k][:, c0:c0 + 512], AF.Square))
                    wait("tensor", "s_s", sq)
                    sqm2[m] = t_mm(pb[1][0:1, 0:512], ones_col[:], sqt[k % 2][:],
                                   start=(k == 0), stop=(k == KD - 1))
                mmd = cur("s_t")
                wait("vector", "s_t", mmd)
                v_op(lambda e, c0=c0, b0=pb[0]: e.tensor_copy(sum1_r[0:1, c0:c0 + 512], b0[0:1, 0:512]))
                v_op(lambda e, c0=c0, b1=pb[1]: e.tensor_copy(sq1_r[0:1, c0:c0 + 512], b1[0:1, 0:512]))
                wait("tensor", "s_v", cur("s_v"))
            return _ln_rows_tail()

        bc_s = sb("bc_s", [128, T], F32)
        bcd = emit_ln_stats_sb(acc)
        wait("vector", "s_aux", bcd)
        v_op(lambda e: e.tensor_tensor(sq1_r[:], sq1_r[:], sq1_r[:], op=ALU.mult))
        v_op(lambda e: e.tensor_scalar(sq1_r[:], sq1_r[:], -EPS, EPS + 1.0,
                                       op0=ALU.mult, op1=ALU.add))
        wait("scalar", "s_v", cur("s_v"))
        sqs = s_op(lambda e: e.activation(sq1_r[:], sq1_r[:], AF.Sqrt))
        wait("vector", "s_s", sqs)
        v_op(lambda e: e.reciprocal(sq1_r[:], sq1_r[:]))
        srow_d = cur("s_v")
        wait("scalar", "s_v", srow_d)
        dma("scalar", "s_aux", row_scr[6:7, :], sq1_r[:])
        wait("scalar", "s_aux", cur("s_aux"))
        dma("scalar", "s_aux", bc_s[:], row_scr[6:7, :].to_broadcast([128, T]))
        sb_done = cur("s_aux")

        wait("vector", "s_aux", sb_done)
        for k in range(KD):
            if k >= 1:
                wait("vector", "s_aux", cur("s_aux"))
            for sl in range(4):
                c0 = sl * 512
                v_op(lambda e, k=k, c0=c0: e.tensor_tensor(
                    xtmp[:], acc[k][:, c0:c0 + 512], bc1_a[:, c0:c0 + 512], op=ALU.mult))
                v_op(lambda e, k=k, c0=c0: e.tensor_tensor(
                    xtmp[:], xtmp[:], bc1_b[:, c0:c0 + 512], op=ALU.add))
                v_op(lambda e, k=k, c0=c0: e.tensor_copy(stg16[0][:, c0:c0 + 512], xtmp[:]))
                v_op(lambda e, k=k, c0=c0: e.tensor_tensor(
                    xtmp[:], xtmp[:], bc_s[:, c0:c0 + 512], op=ALU.mult))
                v_op(lambda e, k=k, c0=c0: e.scalar_tensor_tensor(
                    stg16[1][:, c0:c0 + 512], xtmp[:], ln1w_s[:, k:k + 1],
                    ln1b_s[:, k:k + 1].to_broadcast([128, 512]), op0=ALU.mult, op1=ALU.add))
            xnd = cur("s_v")
            wait("scalar", "s_v", xnd)
            dma("scalar", "s_aux", xn1_scr[k * 128:(k + 1) * 128, :], stg16[0][:])
            dma("scalar", "s_aux", ln1_scr[k * 128:(k + 1) * 128, :], stg16[1][:])
        xn_done = cur("s_aux")
        sb_free("bc_s")

        ckpt('p5b')
        barrier()
        kvw_sb = sb("kvw_sb", [128, 8 * 512], BF16)
        lnS = Streamer("lnSc", [128, 8 * 128], BF16, ("s_wa", "s_wb"))
        kvt16 = sb("kvt16", [128, 16 * 512], BF16)
        dma("sync", "s_c", kvw_sb[:], qkv16d[:, 1024:1536].rearrange("(a p) c -> p a c", p=128))
        wait("sync", "s_aux", xn_done)
        wait("tensor", "s_c", cur("s_c"))
        for bch in range(16):
            b, sem, tgt = lnS.load(ln1_scr[:, bch * 128:(bch + 1) * 128]
                                   .rearrange("(a p) t -> p a t", p=128))
            wait("tensor", sem, tgt)
            bank = pb[4 + bch % 2]
            wait("tensor", "s_s", cur("s_s"))
            for k in range(KD):
                t_mm(bank[:, 0:512], lnS.tiles[b][:, k * 128:(k + 1) * 128],
                     kvw_sb[:, k * 512:(k + 1) * 512], start=(k == 0), stop=(k == KD - 1))
            lnS.consumed(b)
            mmd = cur("s_t")
            wait("scalar", "s_t", mmd)
            s_op(lambda e, bch=bch, bank=bank: e.copy(
                kvt16[:, bch * 512:(bch + 1) * 512], bank[:, 0:512]))
        kv_ev_done = cur("s_s")
        wait("gpsimd", "s_s", kv_ev_done)
        dma("gpsimd", "s_repb", kvloc.rearrange("(b p) c -> p b c", p=128), kvt16[:])
        wait("gpsimd", "s_repb", cur("s_repb"))
        if os.environ.get("KVG_PLAIN"):
            _inc("s_cc", 16)
            emit("gpsimd", lambda e: e.dma_start(
                out=kvg[0:T, :], in_=kvloc).then_inc(sems["s_cc"], 16))
        else:
            _inc("s_cc", 1)
            emit("gpsimd", lambda e: e.collective_compute(
                "AllGather", mybir.AluOpType.bypass,
                replica_groups=[list(range(8))],
                ins=[kvloc.opt()], outs=[kvg.opt()]).then_inc(sems["s_cc"], 1))
        kvg_done = cur("s_aux")
        sb_free("kvt16")
        lnS.free()
        sb_free("kvw_sb")

        ckpt('p5c')
        barrier()
        attn_cat = sb("attn_cat", [128, KD * T], BF16)
        vt16 = sb("vt16", [128, 16 * 256], BF16)
        wq_sb = sb("wq_sb", [128, 4 * 1024], BF16)
        wk_sb = sb("wk_sb", [128, 8 * 256], BF16)
        wv_sb = sb("wv_sb", [128, 8 * 256], BF16)
        xnS = Streamer("xnSd", [128, 8 * 512], BF16, ("s_ba", "s_bb"), n_buf=1)
        dma("sync", "s_c", wk_sb[:], wk16d.rearrange("(a p) c -> p a c", p=128))
        dma("sync", "s_c", wv_sb[:], wv16d.rearrange("(a p) c -> p a c", p=128))
        wait("tensor", "s_c", cur("s_c"))
        qpar = ("s_q0", "s_q1")
        wq_cons = 0
        for sl in range(4):
            b, sem, tgt = xnS.load(xn1_scr[:, sl * 512:(sl + 1) * 512]
                                   .rearrange("(a p) t -> p a t", p=128))
            wait("tensor", sem, tgt)
            for qh2 in range(2):
                if wq_cons:
                    wait("sync", "s_t", wq_cons)
                dma("sync", "s_wa", wq_sb[:],
                    wq16d[:, qh2 * 512:(qh2 + 1) * 512].rearrange("(a p) c -> p a c", p=128))
                wait("tensor", "s_wa", cur("s_wa"))
                for oci in range(4):
                    oc = qh2 * 4 + oci
                    bank = pb[4 + oc % 2]
                    wait("tensor", "s_s", cur("s_s"))
                    for k in range(KD):
                        t_mm(bank[:, 0:512],
                             wq_sb[:, k * 512 + oci * 128:k * 512 + oci * 128 + 128],
                             xnS.tiles[b][:, k * 512:(k + 1) * 512],
                             start=(k == 0), stop=(k == KD - 1))
                    mmd = cur("s_t")
                    wait("scalar", "s_t", mmd)
                    wait("scalar", qpar[oc % 2], cur(qpar[oc % 2]))
                    s_op(lambda e, oc=oc, bank=bank: e.activation(
                        stg16[oc % 2][:, 0:512], bank[:, 0:512], AF.Copy, scale=0.125))
                    wait("scalar", "s_s", cur("s_s"))
                    dma("scalar", qpar[oc % 2], qf_scr[oc * 128:(oc + 1) * 128, sl * 512:(sl + 1) * 512],
                        stg16[oc % 2][:, 0:512])
                wq_cons = cur("s_t")
            for kvh4 in range(4):
                bank = pb[4 + kvh4 % 2]
                wait("tensor", "s_s", cur("s_s"))
                for k in range(KD):
                    t_mm(bank[0:64, 0:512], wk_sb[:, k * 256 + kvh4 * 64:k * 256 + kvh4 * 64 + 64],
                         xnS.tiles[b][:, k * 512:(k + 1) * 512],
                         start=(k == 0), stop=(k == KD - 1))
                mmd = cur("s_t")
                wait("scalar", "s_t", mmd)
                wait("scalar", qpar[kvh4 % 2], cur(qpar[kvh4 % 2]))
                s_op(lambda e, kvh4=kvh4, bank=bank: e.copy(
                    stg16[kvh4 % 2][0:64, 0:512], bank[0:64, 0:512]))
                wait("scalar", "s_s", cur("s_s"))
                dma("scalar", qpar[kvh4 % 2], kf_scr[kvh4 * 64:(kvh4 + 1) * 64, sl * 512:(sl + 1) * 512],
                    stg16[kvh4 % 2][0:64, 0:512])
            for tci in range(4):
                tc = sl * 4 + tci
                bank = pb[4 + tci % 2]
                wait("tensor", "s_s", cur("s_s"))
                for k in range(KD):
                    t_mm(bank[:, 0:256], xnS.tiles[b][:, k * 512 + tci * 128:k * 512 + tci * 128 + 128],
                         wv_sb[:, k * 256:(k + 1) * 256], start=(k == 0), stop=(k == KD - 1))
                mmd = cur("s_t")
                wait("scalar", "s_t", mmd)
                s_op(lambda e, tc=tc, bank=bank: e.copy(
                    vt16[:, tc * 256:(tc + 1) * 256], bank[:, 0:256]))
            xnS.consumed(b)
        proj_done_s = cur("s_s")
        xnS.free()
        sb_free("wv_sb", "wk_sb", "wq_sb")

        ckpt('p5d')
        barrier()
        qhS = Streamer("qhSe", [64, T], BF16, ("s_wa", "s_wb"))
        kfS = Streamer("kfSe", [64, T], BF16, ("s_ba", "s_bb"))
        chi = sb("chi", [128, 16], BF16)
        clo = sb("clo", [128, 16], BF16)
        crows = sb("crows", [2, T], BF16)
        m_t = sb("m_t", [128, 1], F32)
        nm_t = sb("nm_t", [128, 1], F32)
        z_t = sb("z_t", [128, 1], F32)
        cc_t = sb("cc_t", [128, 1], F32)
        pt16 = [sb(f"pt16_{i}", [128, 512], BF16) for i in range(2)]
        ones2 = sb("ones2", [2, 128], BF16)
        v_op(lambda e: e.memset(ones2[:], 1.0))

        psum_free("pb3", "pb2", "pb1", "pb0")
        ps_s = psum("ps_s", [128, 2048], "left")

        wait("tensor", "s_s", proj_done_s)
        wait("scalar", "s_aux", cur("s_aux"))
        for h in range(16):
            kvh = h // 4
            qb_, qsem, qtgt = qhS.load(qf_scr[h * 64:(h + 1) * 64, :])
            wait("tensor", qsem, qtgt)
            qh = qhS.tiles[qb_]
            kb_, ksem, ktgt = kfS.load(kf_scr[kvh * 64:(kvh + 1) * 64, :])
            wait("tensor", ksem, ktgt)
            kfx = kfS.tiles[kb_]
            for i in range(16):
                tl = (i + 1) * 128
                wait("tensor", "s_v", cur("s_v"))
                wait("tensor", "s_s", cur("s_s"))
                for sl in range((tl + 511) // 512):
                    w_ = min(512, tl - sl * 512)
                    t_mm(ps_s[:, sl * 512:sl * 512 + w_],
                         qh[0:64, i * 128:(i + 1) * 128],
                         kfx[0:64, sl * 512:sl * 512 + w_],
                         start=True, stop=True)
                mm1 = cur("s_t")
                wait("vector", "s_t", mm1)
                v_op(lambda e, i=i: e.tensor_tensor(
                    ps_s[:, i * 128:(i + 1) * 128], ps_s[:, i * 128:(i + 1) * 128],
                    mski[:], op=ALU.add))
                vsync()
                v_op(lambda e, tl=tl: e.tensor_reduce(
                    m_t[:], ps_s[:, 0:tl], axis=AX.X, op=ALU.max))
                vsync()
                v_op(lambda e: e.tensor_scalar(nm_t[:], m_t[:], -1.0, None, op0=ALU.mult))
                vs = cur("s_v")
                wait("scalar", "s_v", vs)
                ex = s_op(lambda e, tl=tl: e.activation(
                    ps_s[:, 0:tl], ps_s[:, 0:tl], AF.Exp, bias=nm_t[:, 0:1],
                    accum_out=z_t[:, 0:1]))
                wait("scalar", "s_s", ex)
                ln = s_op(lambda e: e.activation(z_t[:], z_t[:], AF.Ln))
                wait("vector", "s_s", ln)
                v_op(lambda e: e.tensor_tensor(cc_t[:], m_t[:], z_t[:], op=ALU.add))
                vsync()
                v_op(lambda e: e.tensor_scalar(cc_t[:], cc_t[:], -1.0, None, op0=ALU.mult))
                vsync()
                v_op(lambda e, i=i: e.tensor_copy(chi[:, i:i + 1], cc_t[:]))
                vsync()
                v_op(lambda e, i=i: e.scalar_tensor_tensor(
                    clo[:, i:i + 1], chi[:, i:i + 1], -1.0, cc_t[:], op0=ALU.mult, op1=ALU.add))
                wait("tensor", "s_s", ex)
            st_done = cur("s_v")
            wait("scalar", "s_v", st_done)
            dma("scalar", "s_aux", crow_scr[0:1, 0:T].rearrange("a (c p) -> a p c", p=128),
                chi[:], slow=True)
            dma("scalar", "s_aux", crow_scr[1:2, 0:T].rearrange("a (c p) -> a p c", p=128),
                clo[:], slow=True)
            wait("scalar", "s_aux", cur("s_aux"))
            dma("scalar", "s_aux", crows[:, 0:T], crow_scr[:, 0:T])
            crow_d = cur("s_aux")

            for ss in range(4):
                ntc = ss * 4 + 4
                pvbank = pb[6 + h % 2]
                for tc in range(ntc):
                    ptb = pt16[tc % 2]
                    bank = pb[4 + tc % 2]
                    wait("tensor", "s_v", cur("s_v"))
                    t_mm(bank[:, 0:512],
                         kfx[0:64, tc * 128:(tc + 1) * 128],
                         qh[0:64, ss * 512:(ss + 1) * 512],
                         start=True, stop=False)
                    wait("tensor", "s_aux", crow_d)
                    t_mm(bank[:, 0:512], ones2[:, 0:128],
                         crows[:, ss * 512:(ss + 1) * 512], start=False, stop=True)
                    mm2 = cur("s_t")
                    wait("scalar", "s_t", mm2)
                    exv = s_op(lambda e, ptb=ptb, bank=bank: e.activation(
                        ptb[:], bank[:, 0:512], AF.Exp))
                    wait("vector", "s_s", exv)
                    if tc >= ss * 4:
                        sb_i = tc - ss * 4
                        if sb_i > 0:
                            v_op(lambda e, ptb=ptb, sb_i=sb_i: e.memset(
                                ptb[:, 0:sb_i * 128], 0.0))
                        v_op(lambda e, ptb=ptb, sb_i=sb_i: e.tensor_tensor(
                            ptb[:, sb_i * 128:(sb_i + 1) * 128],
                            ptb[:, sb_i * 128:(sb_i + 1) * 128], mskl[:], op=ALU.mult))
                    else:
                        v_op(lambda e: e.tensor_copy(m_t[:], m_t[:]))
                    ptd = cur("s_v")
                    wait("tensor", "s_v", ptd)
                    t_mm(pvbank[0:64, 0:512], vt16[:, tc * 256 + kvh * 64:tc * 256 + kvh * 64 + 64],
                         ptb[:], start=(tc == 0), stop=(tc == ntc - 1))
                pvd = cur("s_t")
                wait("scalar", "s_t", pvd)
                s_op(lambda e, h=h, ss=ss, pvbank=pvbank: e.copy(
                    attn_cat[(h % 2) * 64:(h % 2) * 64 + 64,
                             (h // 2) * T + ss * 512:(h // 2) * T + (ss + 1) * 512],
                    pvbank[0:64, 0:512]))
            qhS.consumed(qb_)
            kfS.consumed(kb_)
        attn_done_s = cur("s_s")

        sb_free("ones2", "pt16_1", "pt16_0", "cc_t", "z_t", "nm_t", "m_t",
                "crows", "clo", "chi", "kfSe1", "kfSe0", "qhSe1", "qhSe0")
        psum_free("ps_s")
        for i in range(4):
            pb[i] = psum(f"pbn{i}", [128, 512], "left")

        ckpt('p5e')
        barrier()
        sb_free("vt16")
        wo_sb = sb("wo_sb", [128, 8 * 1024], BF16)
        dma("sync", "s_c", wo_sb[:], wo16d.rearrange("(a p) c -> p a c", p=128))
        wait("tensor", "s_c", cur("s_c"))
        wait("tensor", "s_s", attn_done_s)
        for oc in range(KD):
            for sl in range(4):
                bankA = pb[0 if (oc * 4 + sl) % 2 == 0 else 2]
                bankB = pb[1 if (oc * 4 + sl) % 2 == 0 else 3]
                wait("tensor", "s_v", cur("s_v"))
                for par, bank in ((0, bankA), (1, bankB)):
                    hs = [h for h in range(16) if h % 2 == par]
                    for j, h in enumerate(hs):
                        t_mm(bank[:, 0:512],
                             wo_sb[par * 64:par * 64 + 64,
                                   (h // 2) * 1024 + oc * 128:(h // 2) * 1024 + oc * 128 + 128],
                             attn_cat[par * 64:par * 64 + 64,
                                      (h // 2) * T + sl * 512:(h // 2) * T + (sl + 1) * 512],
                             start=(j == 0), stop=(j == 7))
                mmd = cur("s_t")
                wait("vector", "s_t", mmd)
                v_op(lambda e, oc=oc, sl=sl, bank=bankA: e.scalar_tensor_tensor(
                    acc[oc][:, sl * 512:(sl + 1) * 512], bank[:, 0:512], 1.0,
                    acc[oc][:, sl * 512:(sl + 1) * 512], op0=ALU.mult, op1=ALU.add))
                v_op(lambda e, oc=oc, sl=sl, bank=bankB: e.scalar_tensor_tensor(
                    acc[oc][:, sl * 512:(sl + 1) * 512], bank[:, 0:512], 1.0,
                    acc[oc][:, sl * 512:(sl + 1) * 512], op0=ALU.mult, op1=ALU.add))
        sb_free("wo_sb", "attn_cat")

        ckpt('p5f')
        barrier()
        qkv_sb2 = sb("qkv_sb2", [128, 8 * 1024], BF16)
        lnG = Streamer("lnSg", [128, 8 * 128], BF16, ("s_wa", "s_wb"))
        kvs = sb("kvs", [128, 8 * 512], BF16)
        qb_t = sb("qb_t", [128, 1024], BF16)
        qk_t = sb("qk_t", [128, 1024], F32)
        sc_t = sb("sc_t", [128, 16 * 8], F32)
        sc_e = sb("sc_e", [128, 16 * 8], F32)
        mx8 = sb("mx8", [128, 16], F32)
        ao_t = sb("ao_t", [128, 1024], F32)
        dma("sync", "s_c", qkv_sb2[:], qkv16d[:, 0:1024].rearrange("(a p) c -> p a c", p=128))
        wait("vector", "s_cc", cur("s_cc"))
        wait("scalar", "s_cc", cur("s_cc"))
        wait("tensor", "s_c", cur("s_c"))
        for bch in range(16):
            b, sem, tgt = lnG.load(ln1_scr[:, bch * 128:(bch + 1) * 128]
                                   .rearrange("(a p) t -> p a t", p=128))
            wait("tensor", sem, tgt)
            wait("tensor", "s_v", cur("s_v"))
            for k in range(KD):
                for half in range(2):
                    t_mm(pb[4 + half][:, 0:512],
                         lnG.tiles[b][:, k * 128:(k + 1) * 128],
                         qkv_sb2[:, k * 1024 + half * 512:k * 1024 + (half + 1) * 512],
                         start=(k == 0), stop=(k == KD - 1))
            lnG.consumed(b)
            mmq = cur("s_t")
            wait("vector", "s_t", mmq)
            v_op(lambda e, b4=pb[4]: e.tensor_copy(qb_t[:, 0:512], b4[:, 0:512]))
            v_op(lambda e, b5=pb[5]: e.tensor_copy(qb_t[:, 512:1024], b5[:, 0:512]))
            if bch > 0:
                wait("scalar", "s_v", kvs_consumed)
            for tp in range(8):
                dma("scalar", "s_aux", kvs[:, tp * 512:(tp + 1) * 512],
                    kvg[tp * T + bch * 128:tp * T + bch * 128 + 128, :])
            kvld = cur("s_aux")
            wait("vector", "s_aux", kvld)
            for tp in range(8):
                kb = kvs[:, tp * 512:tp * 512 + 256].rearrange(
                    "p (g d) -> p g d", g=4).unsqueeze(2).to_broadcast([128, 4, 4, 64])
                v_op(lambda e, kb=kb: e.tensor_tensor(qk_t[:], qb_t[:], kb, op=ALU.mult))
                v_op(lambda e, tp=tp: e.tensor_reduce(
                    sc_t[:, tp:tp + 121:8],
                    qk_t.rearrange("p (h d) -> p h d", d=64), axis=AX.X, op=ALU.add))
            vsync()
            v_op(lambda e: e.tensor_scalar(sc_t[:], sc_t[:], 0.125, None, op0=ALU.mult))
            vsync()
            v_op(lambda e: e.tensor_tensor(
                sc_t.rearrange("p (h c) -> p h c", c=8),
                sc_t.rearrange("p (h c) -> p h c", c=8),
                tmask[:, :].unsqueeze(1).to_broadcast([128, 16, 8]), op=ALU.add))
            vsync()
            v_op(lambda e: e.tensor_reduce(
                mx8[:], sc_t.rearrange("p (h c) -> p h c", c=8), axis=AX.X, op=ALU.max))
            vsync()
            v_op(lambda e: e.tensor_tensor(
                sc_t.rearrange("p (h c) -> p h c", c=8),
                sc_t.rearrange("p (h c) -> p h c", c=8),
                mx8.rearrange("p (h o) -> p h o", o=1).to_broadcast([128, 16, 8]),
                op=ALU.subtract))
            vsm = cur("s_v")
            wait("scalar", "s_v", vsm)
            sex = s_op(lambda e: e.activation(sc_e[:], sc_t[:], AF.Exp))
            wait("vector", "s_s", sex)
            v_op(lambda e: e.tensor_reduce(
                mx8[:], sc_e.rearrange("p (h c) -> p h c", c=8), axis=AX.X, op=ALU.add))
            vsync()
            v_op(lambda e: e.reciprocal(mx8[:], mx8[:]))
            vsync()
            v_op(lambda e: e.tensor_tensor(
                sc_e.rearrange("p (h c) -> p h c", c=8),
                sc_e.rearrange("p (h c) -> p h c", c=8),
                mx8.rearrange("p (h o) -> p h o", o=1).to_broadcast([128, 16, 8]),
                op=ALU.mult))
            vsync()
            for tp in range(8):
                vb = kvs[:, tp * 512 + 256:tp * 512 + 512].rearrange(
                    "p (g d) -> p g d", g=4).unsqueeze(2).to_broadcast([128, 4, 4, 64])
                pbv = sc_e[:, tp:tp + 121:8].rearrange("p h -> p h 1" if False else "p (h o) -> p h o", o=1) \
                    .to_broadcast([128, 16, 64])
                if tp == 0:
                    v_op(lambda e, vb=vb, pbv=pbv: e.tensor_tensor(
                        ao_t.rearrange("p (h d) -> p h d", d=64), vb, pbv, op=ALU.mult))
                else:
                    v_op(lambda e, vb=vb, pbv=pbv: e.tensor_tensor(
                        qk_t.rearrange("p (h d) -> p h d", d=64), vb, pbv, op=ALU.mult))
                    v_op(lambda e: e.tensor_tensor(ao_t[:], ao_t[:], qk_t[:], op=ALU.add))
            kvs_consumed = cur("s_v")
            wait("vector", qpar[bch % 2], cur(qpar[bch % 2]))
            v_op(lambda e, bch=bch: e.tensor_copy(stg16[bch % 2][:, 0:1024], ao_t[:]))
            aod = cur("s_v")
            wait("scalar", "s_v", aod)
            dma("scalar", qpar[bch % 2], at_scr[bch * 128:(bch + 1) * 128, :], stg16[bch % 2][:, 0:1024])
        tiny_st_done = (cur("s_q0"), cur("s_q1"))
        sb_free("ao_t", "mx8", "sc_e", "sc_t", "qk_t", "qb_t", "kvs")
        lnG.free()
        sb_free("qkv_sb2")

        ckpt('p5g')
        barrier()
        atS = Streamer("atSh", [128, 1024], BF16, ("s_wa", "s_wb"))
        wait("sync", "s_q0", tiny_st_done[0])
        wait("sync", "s_q1", tiny_st_done[1])
        for bch in range(16):
            b, sem, tgt = atS.load(at_scr[bch * 128:(bch + 1) * 128, :])
            wait("tensor", sem, tgt)
            for oc in range(KD):
                bank = pb[4 + oc % 2]
                bank16 = bank.bitcast(BF16)
                wait("tensor", "s_s", cur("s_s"))
                t_tr(bank16[:, 0:128], atS.tiles[b][:, oc * 128:(oc + 1) * 128], bf=True)
                trd = cur("s_t")
                wait("scalar", "s_t", trd)
                wait("scalar", qpar[oc % 2], cur(qpar[oc % 2]))
                s_op(lambda e, oc=oc, bank16=bank16: e.copy(
                    stg16[oc % 2][:, 0:128], bank16[:, 0:128]))
                wait("scalar", "s_s", cur("s_s"))
                dma("scalar", qpar[oc % 2], af_scr[oc * 128:(oc + 1) * 128, bch * 128:(bch + 1) * 128],
                    stg16[oc % 2][:, 0:128])
            atS.consumed(b)
        af_done = (cur("s_q0"), cur("s_q1"))
        atS.free()

        ckpt('p5h')
        barrier()
        prj_sb = sb("prj_sb", [128, 8 * 1024], BF16)
        afS = Streamer("afSi", [128, 8 * 512], BF16, ("s_ba", "s_bb"))
        xnI = Streamer("xnIi", [128, 8 * 512], BF16, ("s_sa", "s_sb"), cons_sem="s_v")
        dma("sync", "s_c", prj_sb[:], proj16d.rearrange("(a p) c -> p a c", p=128))
        wait("sync", "s_q0", af_done[0])
        wait("sync", "s_q1", af_done[1])
        wait("tensor", "s_c", cur("s_c"))
        for sl in range(4):
            b, sem, tgt = afS.load(af_scr[:, sl * 512:(sl + 1) * 512]
                                   .rearrange("(a p) t -> p a t", p=128))
            bx, semx, tgtx = xnI.load(xn1_scr[:, sl * 512:(sl + 1) * 512]
                                      .rearrange("(a p) t -> p a t", p=128))
            wait("tensor", sem, tgt)
            wait("vector", semx, tgtx)
            for oc in range(KD):
                bank = pb[4 + oc % 2]
                wait("tensor", "s_v", cur("s_v"))
                for k in range(KD):
                    t_mm(bank[:, 0:512], prj_sb[:, k * 1024 + oc * 128:k * 1024 + oc * 128 + 128],
                         afS.tiles[b][:, k * 512:(k + 1) * 512],
                         start=(k == 0), stop=(k == KD - 1))
                mmd = cur("s_t")
                wait("vector", "s_t", mmd)
                wait("vector", qpar[oc % 2], cur(qpar[oc % 2]))
                v_op(lambda e, oc=oc, bank=bank, bx=bx: e.tensor_tensor(
                    stg32[oc % 2][:], bank[:, 0:512], xnI.tiles[bx][:, oc * 512:(oc + 1) * 512],
                    op=ALU.add))
                h2d = cur("s_v")
                wait("scalar", "s_v", h2d)
                dma("scalar", qpar[oc % 2], h2_scr[oc * 128:(oc + 1) * 128, sl * 512:(sl + 1) * 512],
                    stg32[oc % 2][:])
            afS.consumed(b)
            xnI.cons[bx] = cur("s_v")
        h2_done = (cur("s_q0"), cur("s_q1"))
        xnI.free()
        afS.free()
        sb_free("prj_sb")

        ckpt('p5i')
        barrier()
        h2S = Streamer("h2Sj", [128, 512], F32, ("s_wa", "s_wb"))
        wait("sync", "s_q0", h2_done[0])
        wait("sync", "s_q1", h2_done[1])
        wait("tensor", "s_v", cur("s_v"))
        wait("tensor", "s_s", cur("s_s"))
        sqm3 = {}
        for sl in range(4):
            c0 = sl * 512
            for k in range(KD):
                m = sl * KD + k
                b, sem, tgt = h2S.load(h2_scr[k * 128:(k + 1) * 128, c0:c0 + 512])
                wait("tensor", sem, tgt)
                t_mm(pb[0][0:1, 0:512], ones_col[:], h2S.tiles[b][:],
                     start=(k == 0), stop=(k == KD - 1))
                if m - 2 in sqm3:
                    wait("scalar", "s_t", sqm3[m - 2])
                wait("scalar", sem, tgt)
                sq = s_op(lambda e, b=b, k=k: e.activation(
                    sqt[k % 2][:], h2S.tiles[b][:], AF.Square))
                wait("tensor", "s_s", sq)
                sqm3[m] = t_mm(pb[1][0:1, 0:512], ones_col[:], sqt[k % 2][:],
                               start=(k == 0), stop=(k == KD - 1))
                h2S.consumed(b)
            mmd = cur("s_t")
            wait("vector", "s_t", mmd)
            v_op(lambda e, c0=c0, b0=pb[0]: e.tensor_copy(sum1_r[0:1, c0:c0 + 512], b0[0:1, 0:512]))
            v_op(lambda e, c0=c0, b1=pb[1]: e.tensor_copy(sq1_r[0:1, c0:c0 + 512], b1[0:1, 0:512]))
            wait("tensor", "s_v", cur("s_v"))
        bcd2 = _ln_rows_tail()
        h2F = Streamer("h2Fj", [128, T], F32, ("s_ba", "s_bb"), cons_sem="s_v")
        wait("vector", "s_aux", bcd2)
        for k in range(KD):
            b, sem, tgt = h2F.load(h2_scr[k * 128:(k + 1) * 128, :])
            wait("vector", sem, tgt)
            wait("vector", qpar[k % 2], cur(qpar[k % 2]))
            for sl in range(4):
                c0 = sl * 512
                v_op(lambda e, b=b, c0=c0: e.tensor_tensor(
                    xtmp[:], h2F.tiles[b][:, c0:c0 + 512], bc1_a[:, c0:c0 + 512], op=ALU.mult))
                v_op(lambda e, c0=c0: e.tensor_tensor(
                    xtmp[:], xtmp[:], bc1_b[:, c0:c0 + 512], op=ALU.add))
                v_op(lambda e, k=k, c0=c0: e.scalar_tensor_tensor(
                    stg16[k % 2][:, c0:c0 + 512], xtmp[:], ln2w_s[:, k:k + 1],
                    ln2b_s[:, k:k + 1].to_broadcast([128, 512]), op0=ALU.mult, op1=ALU.add))
            v_op(lambda e, k=k, b=b: e.tensor_tensor(acc[k][:], acc[k][:], h2F.tiles[b][:],
                                                     op=ALU.add))
            h2F.consumed(b)
            lnd = cur("s_v")
            wait("scalar", "s_v", lnd)
            dma("scalar", qpar[k % 2], ln2_scr[k * 128:(k + 1) * 128, :], stg16[k % 2][:])
        ln2_done_aux = (cur("s_q0"), cur("s_q1"))
        h2F.free()
        h2S.free()

        ckpt('p5j')
        barrier()
        rt_sb = sb("rt_sb", [128, 8 * 8], BF16)
        lnR = Streamer("lnRk", [128, 8 * 128], BF16, ("s_wa", "s_wb"))
        gmax = sb("gmax", [128, 1], F32)
        g2t = sb("g2t", [128, 8], F32)
        dma("sync", "s_c", rt_sb[:], rout16d.rearrange("(a p) c -> p a c", p=128))
        wait("sync", "s_q0", ln2_done_aux[0])
        wait("sync", "s_q1", ln2_done_aux[1])
        wait("tensor", "s_c", cur("s_c"))
        for bch in range(16):
            b, sem, tgt = lnR.load(ln2_scr[:, bch * 128:(bch + 1) * 128]
                                   .rearrange("(a p) t -> p a t", p=128))
            wait("tensor", sem, tgt)
            bank = pb[4 + bch % 2]
            wait("tensor", "s_v", cur("s_v"))
            for k in range(KD):
                t_mm(bank[:, 0:8], lnR.tiles[b][:, k * 128:(k + 1) * 128],
                     rt_sb[:, k * 8:(k + 1) * 8], start=(k == 0), stop=(k == KD - 1))
            lnR.consumed(b)
            mmd = cur("s_t")
            wait("vector", "s_t", mmd)
            v_op(lambda e, bch=bch, bank=bank: e.tensor_copy(
                wg[:, bch * 8:(bch + 1) * 8], bank[:, 0:8]))
        for bch in range(16):
            seg = wg[:, bch * 8:(bch + 1) * 8]
            v_op(lambda e, seg=seg: e.tensor_reduce(gmax[:], seg, axis=AX.X, op=ALU.max))
            vsync()
            v_op(lambda e, seg=seg: e.tensor_scalar(
                seg, seg, gmax[:, 0:1], None, op0=ALU.subtract))
            vsm = cur("s_v")
            wait("scalar", "s_v", vsm)
            sx = s_op(lambda e, seg=seg: e.activation(seg, seg, AF.Exp,
                                                      accum_out=gmax[:, 0:1]))
            wait("vector", "s_s", sx)
            v_op(lambda e: e.reciprocal(gmax[:], gmax[:]))
            vsync()
            v_op(lambda e, seg=seg: e.tensor_scalar(
                seg, seg, gmax[:, 0:1], None, op0=ALU.mult))
            vsync()
            v_op(lambda e, seg=seg: e.tensor_reduce(gmax[:], seg, axis=AX.X, op=ALU.max))
            vsync()
            v_op(lambda e, seg=seg: e.tensor_scalar(
                g2t[:], seg, gmax[:, 0:1], None, op0=ALU.is_lt))
            vsync()
            v_op(lambda e, seg=seg: e.tensor_tensor(g2t[:], seg, g2t[:], op=ALU.mult))
            vsync()
            v_op(lambda e: e.tensor_reduce(gmax[:], g2t[:], axis=AX.X, op=ALU.max))
            vsync()
            v_op(lambda e, seg=seg: e.tensor_scalar(
                g2t[:], seg, gmax[:, 0:1], None, op0=ALU.is_ge))
            vsync()
            v_op(lambda e, seg=seg: e.tensor_tensor(seg, seg, g2t[:], op=ALU.mult))
            vsync()
        gate_done = cur("s_v")
        sb_free("g2t", "gmax")
        lnR.free()
        sb_free("rt_sb")

        sb_free("stg32_1", "stg32_0", "stg16_1", "stg16_0", "tmask_sb", "mskl", "mski",
                "ln2b_s", "ln2w_s", "ln1b_s", "ln1w_s", "sqt1", "sqt0", "xtmp",
                "sq1_r", "sum1_r", "bc1_b", "bc1_a")

        ckpt('p5k')
        barrier()
        w1_sb = sb("w1_sb", [128, 8 * 2048], BF16)
        w2_sb = sb("w2_sb", [128, 16 * 1024], BF16)
        he16 = sb("he16", [128, 16 * 512], BF16)
        atok = sb("atok", [128, 4 * 1024], F32)
        ln2q = sb("ln2q", [128, 8 * 512], BF16)

        wait("tensor", "s_v", gate_done)
        w1_consumed = 0
        w2_consumed = 0
        ln2q_consumed = 0
        for tq in range(4):
            q0 = tq * 512
            wait("vector", "s_o0", cur("s_o0"))
            wait("vector", "s_o1", cur("s_o1"))
            if ln2q_consumed:
                wait("sync", "s_t", ln2q_consumed)
            lq = dma("sync", "s_ba", ln2q[:],
                     ln2_scr[:, q0:q0 + 512].rearrange("(a p) t -> p a t", p=128))
            wait("tensor", "s_ba", cur("s_ba"))
            for e_i in range(8):
                wsem = "s_wa" if e_i % 2 == 0 else "s_wb"
                if w1_consumed:
                    wait("sync", "s_t", w1_consumed)
                dma("sync", wsem, w1_sb[:],
                    w116d[e_i * 1024:(e_i + 1) * 1024, :].rearrange("(a p) c -> p a c", p=128))
                if w2_consumed:
                    wait("sync", "s_t", w2_consumed)
                dma("sync", wsem, w2_sb[:],
                    w216d[e_i * 2048:(e_i + 1) * 2048, :].rearrange("(a p) c -> p a c", p=128))
                wld = cur(wsem)
                wait("tensor", wsem, wld)
                ge_tgts = []
                for f in range(16):
                    bank = pb[4 + f % 2]
                    if len(ge_tgts) >= 2:
                        wait("tensor", "s_s", ge_tgts[-2])
                    for k in range(KD):
                        t_mm(bank[:, 0:512], w1_sb[:, k * 2048 + f * 128:k * 2048 + f * 128 + 128],
                             ln2q[:, k * 512:(k + 1) * 512],
                             start=(k == 0), stop=(k == KD - 1))
                    mmd = cur("s_t")
                    wait("scalar", "s_t", mmd)
                    ge_tgts.append(s_op(lambda e, f=f, bank=bank: e.activation(
                        he16[:, f * 512:(f + 1) * 512], bank[:, 0:512], AF.Gelu_apprx_tanh)))
                w1_consumed = cur("s_t")
                wait("tensor", "s_s", cur("s_s"))
                for ts_i in range(4):
                    wait("tensor", "s_v", cur("s_v"))
                    for f in range(16):
                        for half in range(2):
                            t_mm(pb[6 + half][:, 0:512],
                                 he16[:, f * 512 + ts_i * 128:f * 512 + ts_i * 128 + 128],
                                 w2_sb[:, f * 1024 + half * 512:f * 1024 + (half + 1) * 512],
                                 start=(f == 0), stop=(f == 15))
                    mmd = cur("s_t")
                    wait("vector", "s_t", mmd)
                    wcol = wg[:, (tq * 4 + ts_i) * 8 + e_i:(tq * 4 + ts_i) * 8 + e_i + 1]
                    for half in range(2):
                        if e_i == 0:
                            v_op(lambda e, ts_i=ts_i, half=half, wcol=wcol, bk=pb[6 + half]:
                                 e.tensor_scalar(
                                atok[:, ts_i * 1024 + half * 512:ts_i * 1024 + (half + 1) * 512],
                                bk[:, 0:512], wcol, None, op0=ALU.mult))
                        else:
                            v_op(lambda e, ts_i=ts_i, half=half, wcol=wcol, bk=pb[6 + half]:
                                 e.scalar_tensor_tensor(
                                atok[:, ts_i * 1024 + half * 512:ts_i * 1024 + (half + 1) * 512],
                                bk[:, 0:512], wcol,
                                atok[:, ts_i * 1024 + half * 512:ts_i * 1024 + (half + 1) * 512],
                                op0=ALU.mult, op1=ALU.add))
                w2_consumed = cur("s_t")
            ln2q_consumed = cur("s_t")

            for ts_i in range(4):
                osem = "s_o0" if ts_i % 2 == 0 else "s_o1"
                for oc in range(KD):
                    bank = pb[4 + oc % 2]
                    wait("tensor", "s_v", cur("s_v"))
                    t_tr(bank[:, 0:128], acc[oc][:, q0 + ts_i * 128:q0 + (ts_i + 1) * 128])
                    trd = cur("s_t")
                    wait("vector", "s_t", trd)
                    v_op(lambda e, ts_i=ts_i, oc=oc, bank=bank: e.scalar_tensor_tensor(
                        atok[:, ts_i * 1024 + oc * 128:ts_i * 1024 + oc * 128 + 128],
                        bank[:, 0:128], 1.0,
                        atok[:, ts_i * 1024 + oc * 128:ts_i * 1024 + oc * 128 + 128],
                        op0=ALU.mult, op1=ALU.add))
                std = cur("s_v")
                wait("sync", "s_v", std)
                dma("sync", osem, out_hbm[q0 + ts_i * 128:q0 + (ts_i + 1) * 128, :],
                    atok[:, ts_i * 1024:(ts_i + 1) * 1024])
        wait("sync", "s_o0", cur("s_o0"))
        wait("sync", "s_o1", cur("s_o1"))


    except _StopEmission:
        barrier()
        if DUMP == "acc":
            for k in range(KD):
                dma("sync", "s_o0", out_hbm[k * 256:(k + 1) * 256, :]
                    .rearrange("(a b) c -> a (b c)", a=128), acc[k][:])
            wait("sync", "s_o0", cur("s_o0"))
        elif DUMP == "tmask" and "tmask_sb" in ctx:
            dma("sync", "s_o0", out_hbm[0:128, 0:8], tmask[:])
            wait("sync", "s_o0", cur("s_o0"))
        elif DUMP is not None:
            src = {"xn1_scr": xn1_scr, "ln1_scr": ln1_scr, "qf_scr": qf_scr,
                   "kf_scr": kf_scr, "kvloc": kvloc, "kvg": kvg, "at_scr": at_scr,
                   "af_scr": af_scr, "h2_scr": h2_scr, "ln2_scr": ln2_scr}[DUMP]
            if DUMP == "kvg" and cur("s_cc") > 0:
                wait("gpsimd", "s_cc", cur("s_cc"))
            n = 1
            for d in src.shape:
                n *= d
            n = min(n, T * D)
            dma("gpsimd", "s_o0", out_hbm.rearrange("a b -> (a b)")[0:n],
                src.rearrange("a b -> (a b)")[0:n])
            wait("sync", "s_o0", cur("s_o0"))

    with nc.Block() as block:
        @block.sync
        def _(e):
            for f in ops["sync"]:
                f(e)

        @block.scalar
        def _(e):
            for f in ops["scalar"]:
                f(e)

        @block.vector
        def _(e):
            for f in ops["vector"]:
                f(e)

        @block.tensor
        def _(e):
            for f in ops["tensor"]:
                f(e)

        @block.gpsimd
        def _(e):
            for f in ops["gpsimd"]:
                f(e)

    return nc


def kernel(**inp):
    w = host_layout(inp)
    x = np.asarray(inp["L0_attn_hidden_states"], np.float32)
    nc = build_program()
    in_maps = []
    for c in range(8):
        m = dict(w)
        m["x0T"] = np.ascontiguousarray(x[c].T)
        tm = np.zeros((128, 8), np.float32)
        tm[:, c + 1:] = -1e30
        m["tmask"] = tm
        in_maps.append(m)
    res = run_bass_kernel_spmd(nc, in_maps, core_ids=list(range(8)))
    out = np.stack([r["out"] for r in res.results], axis=0)
    return out.astype(np.float32)


# version 06
# speedup vs baseline: 1.0063x; 1.0063x over previous
import os
import numpy as np
import ml_dtypes
import concourse.bass as bass
import concourse.mybir as mybir
from concourse.bass_utils import run_bass_kernel_spmd

F32 = mybir.dt.float32
BF16 = mybir.dt.bfloat16
AF = mybir.ActivationFunctionType
ALU = mybir.AluOpType
AX = mybir.AxisListType
BF = ml_dtypes.bfloat16

T = 2048
D = 1024
KD = D // 128
PP = 16
NN = 16
DTR = 64
TC = 512
NCH = T // TC
WIN_COLS = 5184
EPS = 1e-5

STOP_AFTER = None
TAP_P5G = False
DUMP = None


class _StopEmission(Exception):
    pass


def host_layout(inp):
    g = np.arange(128)
    cols = []
    for p in range(PP):
        cols.extend((g * 16 + p).tolist())
    for p in range(PP):
        for gx in range(32):
            cols.append(2048 + 16 * gx + p)
    for n in range(NN):
        for gx in range(32):
            cols.append(2560 + 16 * gx + n)
    for n in range(NN):
        cols.extend((3072 + g * 16 + n).tolist())
    cols.extend(range(5120, 5184))
    win = np.ascontiguousarray(inp["L0_attn_W_in"][:, cols], dtype=np.float32)

    pg = (np.arange(PP)[:, None] + 16 * g[None, :]).reshape(-1)
    wdt = np.ascontiguousarray(inp["L0_attn_W_dt"][:, pg], dtype=np.float32)
    dtb = np.ascontiguousarray(inp["L0_attn_dt_bias"].reshape(128, 16), np.float32)
    alog = np.ascontiguousarray(inp["L0_attn_A_log"].reshape(128, 256), np.float32)
    dvec = np.ascontiguousarray(inp["L0_attn_D"].reshape(128, 16), np.float32)

    ch = np.array([16 * gx + p for p in range(PP) for gx in range(32)])
    convw = np.ascontiguousarray(
        inp["L0_attn_conv_w"][:, 0, :][ch].reshape(4, 128, 4).transpose(1, 0, 2).reshape(128, 16),
        np.float32)
    convb = np.ascontiguousarray(
        inp["L0_attn_conv_b"][ch].reshape(4, 128).T, np.float32)

    wout = np.ascontiguousarray(inp["L0_attn_W_out"][pg], np.float32)
    ident = np.eye(128, dtype=np.float32)

    rm = inp["L0_ffn_rail_maker"]
    up = inp["L0_ffn_rail_up_proj"]
    bd1 = np.zeros((4, 8, 128, 128), np.float32)
    bd2 = np.zeros((4, 8, 128, 512), np.float32)
    for h in range(4):
        for g8 in range(8):
            for rr in range(8):
                r = 8 * g8 + rr
                bd1[h, g8, 16 * rr:16 * rr + 16, 16 * rr:16 * rr + 16] = rm[h, r]
                bd2[h, g8, 16 * rr:16 * rr + 16, 64 * rr:64 * rr + 64] = up[h, r].T
    wcT = np.ascontiguousarray(inp["L0_ffn_w_cproj"].T, np.float32)

    wqT = np.ascontiguousarray(inp["L1_attn_Wq"].T, np.float32)
    wkT = np.ascontiguousarray(inp["L1_attn_Wk"].T, np.float32)
    wvT = np.ascontiguousarray(inp["L1_attn_Wv"].T, np.float32)
    woT = np.ascontiguousarray(inp["L1_attn_Wo"].T, np.float32)
    qkvT = np.ascontiguousarray(inp["L1_ffn_qkv_weight"].T, np.float32)
    projT = np.ascontiguousarray(inp["L1_ffn_proj_weight"].T, np.float32)
    routT = np.ascontiguousarray(inp["L1_ffn_router_weight"].T, np.float32)
    w1 = np.ascontiguousarray(inp["L1_ffn_moe_w1"].reshape(8 * 1024, 2048), np.float32)
    w2 = np.ascontiguousarray(inp["L1_ffn_moe_w2"].reshape(8 * 2048, 1024), np.float32)
    ln1w = np.ascontiguousarray(inp["L1_ffn_ln1_weight"].reshape(8, 128).T, np.float32)
    ln1b = np.ascontiguousarray(inp["L1_ffn_ln1_bias"].reshape(8, 128).T, np.float32)
    ln2w = np.ascontiguousarray(inp["L1_ffn_ln2_weight"].reshape(8, 128).T, np.float32)
    ln2b = np.ascontiguousarray(inp["L1_ffn_ln2_bias"].reshape(8, 128).T, np.float32)
    ii, jj = np.indices((128, 128))
    mask_inf = np.where(jj > ii, np.float32(-1e30), np.float32(0.0))
    mask_low = (jj <= ii).astype(np.float32).T
    return {
        "win": win, "wdt": wdt, "dtb": dtb, "alog": alog, "dvec": dvec,
        "convw": convw, "convb": convb, "wout": wout, "ident": ident,
        "bd1": np.ascontiguousarray(bd1.reshape(4096, 128)), "bd2": np.ascontiguousarray(bd2.reshape(4096, 512)), "wcT": wcT,
        "wqT": wqT, "wkT": wkT, "wvT": wvT, "woT": woT, "qkvT": qkvT,
        "projT": projT, "routT": routT, "w1": w1, "w2": w2,
        "ln1w": ln1w, "ln1b": ln1b, "ln2w": ln2w, "ln2b": ln2b,
        "mask_inf": mask_inf, "mask_low": mask_low,
    }


def build_program():
    nc = bass.Bass("TRN2", target_bir_lowering=False, detect_race_conditions=False)

    def ckpt(name):
        if STOP_AFTER == name:
            raise _StopEmission()

    def din(name, shape):
        return nc.dram_tensor(name, list(shape), F32, kind="ExternalInput").ap()

    x0T = din("x0T", (D, T))
    win = din("win", (D, WIN_COLS))
    wdt = din("wdt", (DTR, 2048))
    dtb = din("dtb", (128, 16))
    alog = din("alog", (128, 256))
    dvec = din("dvec", (128, 16))
    convw = din("convw", (128, 16))
    convb = din("convb", (128, 4))
    wout = din("wout", (2048, D))
    ident_in = din("ident", (128, 128))
    bd1_in = din("bd1", (4 * 8 * 128, 128))
    bd2_in = din("bd2", (4 * 8 * 128, 512))
    wcT_in = din("wcT", (16384, D))
    wqT_in = din("wqT", (D, D))
    wkT_in = din("wkT", (D, 256))
    wvT_in = din("wvT", (D, 256))
    woT_in = din("woT", (D, D))
    qkvT_in = din("qkvT", (D, 1536))
    projT_in = din("projT", (D, D))
    routT_in = din("routT", (D, 8))
    w1_in = din("w1", (8 * 1024, 2048))
    w2_in = din("w2", (8 * 2048, 1024))
    ln1w_in = din("ln1w", (128, 8))
    ln1b_in = din("ln1b", (128, 8))
    ln2w_in = din("ln2w", (128, 8))
    ln2b_in = din("ln2b", (128, 8))
    mski_in = din("mask_inf", (128, 128))
    tmask_in = din("tmask", (128, 8))
    mskl_in = din("mask_low", (128, 128))

    out_hbm = nc.dram_tensor("out", [T, D], F32, kind="ExternalOutput").ap()
    dbg_out = nc.dram_tensor("dbg_out", [128, 5264], F32, kind="ExternalOutput").ap() if TAP_P5G else None

    win16 = nc.dram_tensor("win16_scr", [D, WIN_COLS], BF16).ap()
    wout16 = nc.dram_tensor("wout16_scr", [2048, D], BF16).ap()
    bd1_16 = nc.dram_tensor("bd1_scr", [4 * 8 * 128, 128], BF16).ap()
    bd2_16 = nc.dram_tensor("bd2_scr", [4 * 8 * 128, 512], BF16).ap()
    wc16 = nc.dram_tensor("wc16_scr", [16384, D], BF16).ap()
    wq16d = nc.dram_tensor("wq16_scr", [D, D], BF16).ap()
    wk16d = nc.dram_tensor("wk16_scr", [D, 256], BF16).ap()
    wv16d = nc.dram_tensor("wv16_scr", [D, 256], BF16).ap()
    wo16d = nc.dram_tensor("wo16_scr", [D, D], BF16).ap()
    qkv16d = nc.dram_tensor("qkv16_scr", [D, 1536], BF16).ap()
    proj16d = nc.dram_tensor("proj16_scr", [D, D], BF16).ap()
    rout16d = nc.dram_tensor("rout16_scr", [D, 8], BF16).ap()
    w116d = nc.dram_tensor("w116_scr", [8 * 1024, 2048], BF16).ap()
    w216d = nc.dram_tensor("w216_scr", [8 * 2048, 1024], BF16).ap()
    crow_scr = nc.dram_tensor("crow_scr", [2, T], BF16).ap()
    xn1_scr = nc.dram_tensor("xn1_scr", [D, T], BF16).ap()
    ln1_scr = nc.dram_tensor("ln1_scr", [D, T], BF16).ap()
    ln2_scr = nc.dram_tensor("ln2_scr", [D, T], BF16).ap()
    qf_scr = nc.dram_tensor("qf_scr", [D, T], BF16).ap()
    kf_scr = nc.dram_tensor("kf_scr", [256, T], BF16).ap()
    at_scr = nc.dram_tensor("at_scr", [T, D], BF16).ap()
    af_scr = nc.dram_tensor("af_scr", [D, T], BF16).ap()
    h2_scr = nc.dram_tensor("h2_scr", [D, T], F32).ap()
    kvloc = nc.dram_tensor("kvloc", [T, 512], BF16).ap()
    kvg = nc.dram_tensor("kvg", [8 * T, 512], BF16, addr_space="Shared").ap()
    xnb_scr = nc.dram_tensor("xnb_scr", [D, T], BF16).ap()
    row_scr = nc.dram_tensor("row_scr", [8, T], F32).ap()

    ctx = {}
    def sb(name, shape, dtype, side="left"):
        cm = nc.sbuf_tensor(name, list(shape), dtype, side=side)
        t = cm.__enter__(); ctx[name] = cm; return t

    def sb_free(*names):
        for n in names:
            ctx.pop(n).__exit__(None, None, None)

    acc = [sb(f"acc{i}", [128, T], F32) for i in range(KD)]
    negA = sb("negA", [128, 256], F32)
    dtb2 = sb("dtb2", [128, 16], F32)
    dvec_s = sb("dvec_s", [128, 16], F32)
    convw_s = sb("convw_s", [128, 16], F32)
    convb_s = sb("convb_s", [128, 4], F32)
    wdt16 = sb("wdt16", [64, 2048], BF16)
    ones_col = sb("ones_col", [128, 1], F32)
    ident = sb("ident_sb", [128, 128], F32)
    ident16 = sb("ident16_sb", [128, 128], BF16)
    eps_c = sb("eps_c", [1, 1], F32)
    h_carry = sb("h_carry", [128, 256], F32)


    WBL = 256
    win_blk = [sb(f"win_blk{i}", [128, 8 * WBL], BF16) for i in range(2)]
    xnb_ch = sb("xnb_ch", [128, KD * TC], BF16)
    siluz = [sb(f"siluz{i}", [128, TC], BF16) for i in range(PP)]
    y2b = siluz
    xpre = [sb(f"xpre{i}", [128, TC + 4], F32) for i in range(4)]
    xcs = [sb(f"xcs{i}", [128, TC], F32) for i in range(4)]
    braw16 = sb("braw16", [128, 4 * TC], BF16)
    brep = [sb(f"brep{i}", [128, TC], BF16) for i in range(NN)]
    cfull = [sb(f"cfull{i}", [128, TC], BF16) for i in range(NN)]
    dtr16 = sb("dtr16", [64, TC], BF16)

    bc_a = sb("bc_a", [128, T], F32, side="right")
    bc_b = sb("bc_b", [128, T], F32, side="right")
    sq_t = [sb(f"sq_t{i}", [128, 512], F32, side="right") for i in range(2)]
    sum_r = sb("sum_r", [1, T], F32, side="right")
    sq_r = sb("sq_r", [1, T], F32, side="right")

    pctx = {}
    def psum(name, shape, side):
        cm = nc.psum_tensor(name, list(shape), F32, side=side)
        t = cm.__enter__(); pctx[name] = cm; return t
    def psum_free(*names):
        for n in names:
            pctx.pop(n).__exit__(None, None, None)
    pb = [psum(f"pb{i}", [128, 512], "left" if i < 4 else "right")
          for i in range(8)]

    sems = {}
    for s in ("s_c", "s_x", "s_ph2", "s_aux", "s_nb", "s_wa", "s_wb", "s_sa", "s_sb",
              "s_repb", "s_r0", "s_r1", "s_q0", "s_q1", "s_o0", "s_o1", "s_cast", "s_cc",
              "s_wca", "s_wcb", "s_ba", "s_bb", "s_t", "s_v", "s_s"):
        sems[s] = nc.semaphore(s).__enter__()

    ops = {e: [] for e in ("sync", "scalar", "vector", "tensor", "gpsimd")}
    cnt = {}

    def _inc(name, by=1):
        cnt[name] = cnt.get(name, 0) + by
        return cnt[name]

    def cur(name):
        return cnt.get(name, 0)

    def emit(eng, fn):
        ops[eng].append(fn)

    def wait(eng, sem_name, val):
        if val <= 0:
            return
        emit(eng, lambda e, s=sems[sem_name], v=val: e.wait_ge(s, v))

    def t_mm(out_ap, lhsT, rhs, start, stop):
        tgt = _inc("s_t")
        emit("tensor", lambda e, o=out_ap, l=lhsT, r=rhs, st=start, sp=stop:
             e.matmul(o, l, r, start=st, stop=sp).then_inc(sems["s_t"], 1))
        return tgt

    def t_tr(out_ap, in_ap, bf=False):
        tgt = _inc("s_t")
        idt = ident16 if bf else ident
        emit("tensor", lambda e, o=out_ap, i=in_ap, idt=idt:
             e.transpose(o, i, idt[:]).then_inc(sems["s_t"], 1))
        return tgt

    def v_op(fn):
        tgt = _inc("s_v")
        emit("vector", lambda e, fn=fn: fn(e).then_inc(sems["s_v"], 1))
        return tgt

    def s_op(fn):
        tgt = _inc("s_s")
        emit("scalar", lambda e, fn=fn: fn(e).then_inc(sems["s_s"], 1))
        return tgt

    def dma(eng, sem_name, out_ap, in_ap, slow=False):
        tgt = _inc(sem_name, 16)
        if slow:
            emit(eng, lambda e, o=out_ap, i=in_ap, s=sems[sem_name]:
                 e.dma_start(out=o, in_=i, allow_slow_non_contiguous=True).then_inc(s, 16))
        else:
            emit(eng, lambda e, o=out_ap, i=in_ap, s=sems[sem_name]:
                 e.dma_start(out=o, in_=i).then_inc(s, 16))
        return tgt

    def vsync():
        wait("vector", "s_v", cur("s_v"))

    def barrier():
        snap = [(n, cur(n)) for n in ("s_t", "s_v", "s_s", "s_c", "s_x", "s_ph2",
                                      "s_aux", "s_nb", "s_wa", "s_wb", "s_sa", "s_sb",
                                      "s_repb", "s_r0", "s_r1", "s_q0", "s_q1",
                                      "s_wca", "s_wcb", "s_ba", "s_bb")]
        own = {"tensor": "s_t", "vector": "s_v", "scalar": "s_s"}
        for eng in ("sync", "scalar", "vector", "tensor", "gpsimd"):
            for n, v in snap:
                if v > 0 and own.get(eng) != n:
                    wait(eng, n, v)

    try:
        dma("sync", "s_c", negA[:], alog)
        dma("sync", "s_c", dtb2[:], dtb)
        dma("sync", "s_c", dvec_s[:], dvec)
        dma("sync", "s_c", convw_s[:], convw)
        dma("sync", "s_c", convb_s[:], convb)
        dma("sync", "s_c", bc_a[0:64, 0:2048], wdt)
        dma("sync", "s_c", ident[:], ident_in)
        ld0 = cur("s_c")

        wait("vector", "s_c", ld0)
        v_op(lambda e: e.memset(ones_col[:], 1.0))
        v_op(lambda e: e.memset(eps_c[:], EPS))
        v_op(lambda e: e.tensor_copy(ident16[:], ident[:]))
        v_op(lambda e: e.memset(h_carry[:], 0.0))
        v_op(lambda e: e.tensor_copy(wdt16[:], bc_a[0:64, 0:2048]))
        v_op(lambda e: e.tensor_scalar(dtb2[:], dtb2[:], 2.0, None, op0=ALU.mult))
        for i in range(4):
            v_op(lambda e, i=i: e.memset(xpre[i][:, 0:4], 0.0))
        wait("scalar", "s_c", ld0)
        s_op(lambda e: e.activation(negA[:], negA[:], AF.Exp))
        wait("vector", "s_s", cur("s_s"))
        v_op(lambda e: e.tensor_scalar(negA[:], negA[:], -1.0, None, op0=ALU.mult))

        for k in range(KD):
            dma("sync", "s_x", acc[k][:], x0T[k * 128:(k + 1) * 128, :])
        ld_x = cur("s_x")

        wait("tensor", "s_x", ld_x)
        wait("tensor", "s_v", cur("s_v"))
        wait("scalar", "s_x", ld_x)
        sq_mm = {}
        for sl in range(4):
            c0 = sl * 512
            for k in range(KD):
                t_mm(pb[0][0:1, 0:512], ones_col[:], acc[k][:, c0:c0 + 512],
                     start=(k == 0), stop=(k == KD - 1))
            for k in range(KD):
                m = sl * KD + k
                if m - 2 in sq_mm:
                    wait("scalar", "s_t", sq_mm[m - 2])
                sq = s_op(lambda e, k=k, c0=c0: e.activation(
                    sq_t[k % 2][:], acc[k][:, c0:c0 + 512], AF.Square))
                wait("tensor", "s_s", sq)
                sq_mm[m] = t_mm(pb[1][0:1, 0:512], ones_col[:], sq_t[k % 2][:],
                                start=(k == 0), stop=(k == KD - 1))
            mmd = cur("s_t")
            wait("vector", "s_t", mmd)
            v_op(lambda e, c0=c0, b0=pb[0]: e.tensor_copy(sum_r[0:1, c0:c0 + 512], b0[0:1, 0:512]))
            v_op(lambda e, c0=c0, b1=pb[1]: e.tensor_copy(sq_r[0:1, c0:c0 + 512], b1[0:1, 0:512]))
            wait("tensor", "s_v", cur("s_v"))

        v_op(lambda e: e.tensor_scalar(sum_r[:], sum_r[:], 1.0 / D, None, op0=ALU.mult))
        v_op(lambda e: e.tensor_scalar(sq_r[:], sq_r[:], 1.0 / D, None, op0=ALU.mult))
        v_op(lambda e: e.tensor_tensor(bc_a[0:1, :], sum_r[:], sum_r[:], op=ALU.mult))
        v_op(lambda e: e.tensor_tensor(sq_r[:], sq_r[:], bc_a[0:1, :], op=ALU.subtract))
        wait("scalar", "s_v", cur("s_v"))
        rs = s_op(lambda e: e.activation(sq_r[:], sq_r[:], AF.Sqrt, bias=eps_c[0:1, 0:1]))
        wait("vector", "s_s", rs)
        v_op(lambda e: e.reciprocal(sq_r[:], sq_r[:]))
        v_op(lambda e: e.tensor_tensor(sum_r[:], sum_r[:], sq_r[:], op=ALU.mult))
        v_op(lambda e: e.tensor_scalar(sum_r[:], sum_r[:], -1.0, None, op0=ALU.mult))
        rows_done = cur("s_v")

        wait("scalar", "s_v", rows_done)
        dma("scalar", "s_aux", row_scr[4:5, :], sq_r[:])
        dma("scalar", "s_aux", row_scr[5:6, :], sum_r[:])
        wait("scalar", "s_aux", cur("s_aux"))
        dma("scalar", "s_aux", bc_a[:], row_scr[4:5, :].to_broadcast([128, T]))
        dma("scalar", "s_aux", bc_b[:], row_scr[5:6, :].to_broadcast([128, T]))
        bc_done = cur("s_aux")

        wait("vector", "s_aux", bc_done)
        for k in range(KD):
            for sl in range(4):
                c0 = sl * 512
                v_op(lambda e, k=k, c0=c0, b=sl % 2: e.tensor_tensor(
                    xcs[b][:], acc[k][:, c0:c0 + 512], bc_a[:, c0:c0 + 512], op=ALU.mult))
                v_op(lambda e, k=k, c0=c0, b=sl % 2: e.tensor_tensor(
                    braw16[:, c0:c0 + 512], xcs[b][:], bc_b[:, c0:c0 + 512], op=ALU.add))
            cst = cur("s_v")
            wait("scalar", "s_v", cst)
            dma("scalar", "s_aux", xnb_scr[k * 128:(k + 1) * 128, :], braw16[:, 0:T])
            wait("vector", "s_aux", cur("s_aux"))

        dma("gpsimd", "s_cast", win16, win)
        dma("gpsimd", "s_cast", wout16, wout)
        dma("gpsimd", "s_cast", bd1_16, bd1_in)
        dma("gpsimd", "s_cast", bd2_16, bd2_in)
        for q in range(4):
            dma("gpsimd", "s_cast", wc16[q * 4096:(q + 1) * 4096, :],
                wcT_in[q * 4096:(q + 1) * 4096, :])
        wait("sync", "s_cast", cur("s_cast"))

        barrier()
        sb_free("sq_r", "sum_r", "sq_t1", "sq_t0", "bc_b", "bc_a")

        xrep_t = [sb(f"xrep{i}", [128, TC], F32, side="right") for i in range(2)]
        sp_x = sb("sp_x", [128, TC], F32, side="right")
        sp_r = sb("sp_r", [128, TC], F32, side="right")
        sp_t = sb("sp_t", [128, TC], F32, side="right")
        delta_t = sb("delta_t", [128, TC], F32, side="right")
        dx_t = sb("dx_t", [128, TC], F32, side="right")
        a_t = [sb(f"a_t{i}", [128, TC], F32, side="right") for i in range(2)]
        bb_t = [sb(f"bb_t{i}", [128, TC], F32, side="right") for i in range(2)]
        h_t = [sb(f"h_t{i}", [128, TC], F32, side="right") for i in range(2)]
        prod_t = sb("prod_t", [128, TC], F32, side="right")
        y_p = sb("y_p", [128, TC], F32, side="right")
        wsl = [sb(f"wsl{i}", [128, D], BF16, side="right") for i in range(2)]

        ckpt('ln0')
        for c in range(NCH):
            t0 = c * TC
            for k in range(KD):
                dma("sync", "s_nb", xnb_ch[:, k * TC:(k + 1) * TC],
                    xnb_scr[k * 128:(k + 1) * 128, t0:t0 + TC])
            ld_x0 = cur("s_nb")

            n_blocks = 21
            evac_tgts = []
            blk_mm_done = []
            for j in range(n_blocks):
                bw = WBL if j < 20 else 64
                buf = j % 2
                if j >= 2:
                    wait("sync", "s_t", blk_mm_done[j - 2])
                wsem = "s_wa" if buf == 0 else "s_wb"
                for k in range(KD):
                    dma("sync", wsem, win_blk[buf][:, k * WBL:k * WBL + bw],
                        win16[k * 128:(k + 1) * 128, j * WBL:j * WBL + bw])
                wait("tensor", wsem, cur(wsem))
                wait("tensor", "s_nb", ld_x0)
                n_tiles = 2 if j < 20 else 1
                for oti in range(n_tiles):
                    ot = 2 * j + oti
                    w_ = 128 if j < 20 else 64
                    bank = pb[ot % 2]
                    if len(evac_tgts) >= 2:
                        wait("tensor", "s_s", evac_tgts[-2])
                    for k in range(KD):
                        t_mm(bank[0:w_, 0:TC],
                             win_blk[buf][:, k * WBL + oti * 128:k * WBL + oti * 128 + w_],
                             xnb_ch[:, k * TC:(k + 1) * TC],
                             start=(k == 0), stop=(k == KD - 1))
                    mmd = cur("s_t")
                    wait("scalar", "s_t", mmd)
                    ps = bank[0:w_, 0:TC]
                    if ot < 16:
                        tgt = s_op(lambda e, i=ot, ps=ps: e.activation(siluz[i][:], ps, AF.Silu))
                    elif ot < 20:
                        tgt = s_op(lambda e, i=ot - 16, ps=ps: e.copy(xpre[i][:, 4:4 + TC], ps))
                    elif ot < 24:
                        tgt = s_op(lambda e, i=ot - 20, ps=ps: e.copy(
                            braw16[:, i * TC:(i + 1) * TC], ps))
                    elif ot < 40:
                        tgt = s_op(lambda e, i=ot - 24, ps=ps: e.copy(cfull[i][:], ps))
                    else:
                        tgt = s_op(lambda e, ps=ps: e.copy(dtr16[:], ps))
                    evac_tgts.append(tgt)
                blk_mm_done.append(cur("s_t"))
            evac_all = cur("s_s")

            wait("gpsimd", "s_s", evac_all)
            brep_tgts = []
            for n in range(NN):
                src = braw16[32 * (n % 4):32 * (n % 4) + 32, (n // 4) * TC:(n // 4 + 1) * TC]
                brep_tgts.append(dma("gpsimd", "s_repb",
                                     brep[n][:], src.unsqueeze(1).to_broadcast([32, 4, TC])))

            wait("vector", "s_s", evac_all)
            for i in range(4):
                v_op(lambda e, i=i: e.tensor_scalar(
                    xcs[i][:], xpre[i][:, 1:1 + TC], convw_s[:, 4 * i:4 * i + 1], None, op0=ALU.mult))
                for kk in range(1, 4):
                    v_op(lambda e, i=i, kk=kk: e.scalar_tensor_tensor(
                        xcs[i][:], xpre[i][:, 1 + kk:1 + kk + TC],
                        convw_s[:, 4 * i + kk:4 * i + kk + 1], xcs[i][:],
                        op0=ALU.mult, op1=ALU.add))
            convv = cur("s_v")
            wait("scalar", "s_v", convv)
            for i in range(4):
                s_op(lambda e, i=i: e.activation(xcs[i][:], xcs[i][:], AF.Silu,
                                                 bias=convb_s[:, i:i + 1]))
            conv_done = cur("s_s")
            wait("vector", "s_s", conv_done)
            for i in range(4):
                v_op(lambda e, i=i: e.tensor_copy(xpre[i][:, 1:4], xpre[i][:, 1 + TC:4 + TC]))

            xrep_consumer = {}
            prev_delta_v = 0
            for pi in range(PP):
                pbuf = pi % 2
                wait("gpsimd", "s_s", conv_done)
                if pi >= 2:
                    wait("gpsimd", "s_v", xrep_consumer[pi - 2])
                xr = dma("gpsimd", "s_r0" if pbuf == 0 else "s_r1", xrep_t[pbuf][:],
                         xcs[pi // 4][32 * (pi % 4):32 * (pi % 4) + 32, :]
                         .unsqueeze(1).to_broadcast([32, 4, TC]))
                bank = pb[2 + pbuf]
                wait("tensor", "s_s", cur("s_s"))
                t_mm(bank[:, 0:TC], wdt16[:, pi * 128:(pi + 1) * 128], dtr16[:],
                     start=True, stop=True)
                dmm = cur("s_t")
                wait("scalar", "s_t", dmm)
                wait("scalar", "s_v", prev_delta_v)
                s_op(lambda e, bank=bank: e.copy(sp_x[:], bank[:, 0:TC]))
                s_op(lambda e, pi=pi: e.activation(sp_r[:], sp_x[:], AF.Relu,
                                                   bias=dtb2[:, pi:pi + 1]))
                wait("vector", "s_s", cur("s_s"))
                v_op(lambda e: e.scalar_tensor_tensor(
                    sp_t[:], sp_r[:], -2.0, sp_x[:], op0=ALU.mult, op1=ALU.add))
                wait("scalar", "s_v", cur("s_v"))
                s_op(lambda e, pi=pi: e.activation(sp_t[:], sp_t[:], AF.Exp,
                                                   bias=dtb2[:, pi:pi + 1]))
                s_op(lambda e: e.activation(sp_t[:], sp_t[:], AF.Ln, bias=ones_col[:, 0:1]))
                wait("vector", "s_s", cur("s_s"))
                v_op(lambda e: e.tensor_tensor(delta_t[:], sp_r[:], sp_t[:], op=ALU.add))

                prev_delta_v = cur("s_v")
                wait("vector", "s_r0" if pbuf == 0 else "s_r1", xr)

                v_op(lambda e, pbuf=pbuf: e.tensor_tensor(
                    dx_t[:], delta_t[:], xrep_t[pbuf][:], op=ALU.mult))

                if pi == 0:
                    wait("vector", "s_repb", brep_tgts[-1])
                for n in range(NN):
                    nb = n % 2
                    idx = pi * 16 + n
                    av = v_op(lambda e, nb=nb, idx=idx: e.tensor_scalar(
                        a_t[nb][:], delta_t[:], negA[:, idx:idx + 1], None, op0=ALU.mult))
                    wait("scalar", "s_v", av)
                    ae = s_op(lambda e, nb=nb: e.activation(a_t[nb][:], a_t[nb][:], AF.Exp))
                    v_op(lambda e, nb=nb, n=n: e.tensor_tensor(
                        bb_t[nb][:], dx_t[:], brep[n][:], op=ALU.mult))
                    wait("vector", "s_s", ae)

                    sc_t = v_op(lambda e, nb=nb, idx=idx: e.tensor_tensor_scan(
                        h_t[nb][:], a_t[nb][:], bb_t[nb][:], h_carry[:, idx:idx + 1],
                        op0=ALU.mult, op1=ALU.add))
                    wait("vector", "s_v", sc_t)

                    v_op(lambda e, nb=nb, idx=idx: e.tensor_copy(
                        h_carry[:, idx:idx + 1], h_t[nb][:, TC - 1:TC]))
                    if n == 0:
                        v_op(lambda e, nb=nb, n=n: e.tensor_tensor(
                            y_p[:], h_t[nb][:], cfull[n][:], op=ALU.mult))
                    else:
                        v_op(lambda e, nb=nb, n=n: e.tensor_tensor(
                            prod_t[:], h_t[nb][:], cfull[n][:], op=ALU.mult))
                        v_op(lambda e: e.tensor_tensor(y_p[:], y_p[:], prod_t[:], op=ALU.add))
                v_op(lambda e, pbuf=pbuf, pi=pi: e.scalar_tensor_tensor(
                    prod_t[:], xrep_t[pbuf][:], dvec_s[:, pi:pi + 1], y_p[:],
                    op0=ALU.mult, op1=ALU.add))
                xrep_consumer[pi] = cur("s_v")
                v_op(lambda e, pi=pi: e.tensor_tensor(
                    y2b[pi][:], prod_t[:], siluz[pi][:], op=ALU.mult))
            scan_done = cur("s_v")

            t_base = cur("s_t")
            slab_tgts = []
            for pi in range(PP):
                if pi >= 2:
                    wait("sync", "s_t", t_base + 8 * (pi - 1))
                slab_tgts.append((("s_sa" if pi % 2 == 0 else "s_sb"),
                                  dma("sync", "s_sa" if pi % 2 == 0 else "s_sb",
                                      wsl[pi % 2][:], wout16[pi * 128:(pi + 1) * 128, :])))

            wait("tensor", "s_v", scan_done)
            wait("tensor", "s_s", cur("s_s"))
            for pi in range(PP):
                wait("tensor", slab_tgts[pi][0], slab_tgts[pi][1])
                for o in range(8):
                    t_mm(pb[o][:, 0:TC], wsl[pi % 2][:, o * 128:(o + 1) * 128], y2b[pi][:],
                         start=(pi == 0), stop=(pi == PP - 1))
            wout_done = cur("s_t")
            wait("vector", "s_t", wout_done)
            for o in range(8):
                v_op(lambda e, o=o, t0=t0, bk=pb[o]: e.scalar_tensor_tensor(
                    acc[o][:, t0:t0 + TC], bk[:, 0:TC], 1.0, acc[o][:, t0:t0 + TC],
                    op0=ALU.mult, op1=ALU.add))
            barrier()

        ckpt('mamba')
        sb_free("wsl1", "wsl0", "y_p", "prod_t", "h_t1", "h_t0", "bb_t1", "bb_t0",
                "a_t1", "a_t0", "dx_t", "delta_t", "sp_t", "sp_r", "sp_x",
                "xrep1", "xrep0")
        sb_free("dtr16", *[f"cfull{i}" for i in reversed(range(NN))],
                *[f"brep{i}" for i in reversed(range(NN))], "braw16",
                *[f"xcs{i}" for i in reversed(range(4))],
                *[f"xpre{i}" for i in reversed(range(4))],
                *[f"siluz{i}" for i in reversed(range(PP))], "xnb_ch",
                "win_blk1", "win_blk0")

        dma("gpsimd", "s_cast", wq16d, wqT_in)
        dma("gpsimd", "s_cast", wk16d, wkT_in)
        dma("gpsimd", "s_cast", wv16d, wvT_in)
        dma("gpsimd", "s_cast", wo16d, woT_in)
        dma("gpsimd", "s_cast", qkv16d, qkvT_in)
        dma("gpsimd", "s_cast", proj16d, projT_in)
        dma("gpsimd", "s_cast", rout16d, routT_in)
        for q in range(4):
            dma("gpsimd", "s_cast", w116d[q * 2048:(q + 1) * 2048, :],
                w1_in[q * 2048:(q + 1) * 2048, :])
            dma("gpsimd", "s_cast", w216d[q * 4096:(q + 1) * 4096, :],
                w2_in[q * 4096:(q + 1) * 4096, :])

        TF = 256
        xnb_t = sb("xnb_t", [128, KD * TF], BF16)
        y_t = sb("y_t", [128, 32 * TF], BF16)
        h_half = sb("h_half", [128, 64 * TF], BF16)
        wc_sb = [sb(f"wc_sb{i}", [128, 8 * 1024], BF16) for i in range(2)]
        bd1_sb = sb("bd1_sb", [128, 32 * 128], BF16)
        bd2_sb = [sb(f"bd2_sb{i}", [128, 512], BF16) for i in range(2)]

        bd1_ld = dma("sync", "s_c", bd1_sb[:],
                     bd1_16.rearrange("(a p) c -> p a c", p=128))
        wait("tensor", "s_c", bd1_ld)

        s1_mm_end = 0
        s2_mm_end = 0
        for it in range(8):
            c0 = it * TF
            if it >= 1:
                wait("sync", "s_t", s1_mm_end)
            for k in range(KD):
                dma("sync", "s_nb", xnb_t[:, k * TF:(k + 1) * TF],
                    xnb_scr[k * 128:(k + 1) * 128, c0:c0 + TF])
            ld_t = cur("s_nb")
            wait("tensor", "s_nb", ld_t)
            wait("scalar", "s_t", s2_mm_end)
            ev_prev = []
            for blk in range(32):
                h, g8 = divmod(blk, 8)
                bank = pb[blk % 2]
                if len(ev_prev) >= 2:
                    wait("tensor", "s_s", ev_prev[-2])
                if blk < 2:
                    wait("tensor", "s_v", cur("s_v"))
                t_mm(bank[:, 0:TF], bd1_sb[:, blk * 128:(blk + 1) * 128],
                     xnb_t[:, g8 * TF:(g8 + 1) * TF], start=True, stop=True)
                mmd = cur("s_t")
                wait("scalar", "s_t", mmd)
                ev_prev.append(s_op(lambda e, blk=blk, bank=bank: e.copy(
                    y_t[:, blk * TF:(blk + 1) * TF], bank[:, 0:TF])))
            s1_mm_end = cur("s_t")
            s1_done = cur("s_s")

            for half in range(2):
                wait("tensor", "s_s", s1_done)
                wait("tensor", "s_v", cur("s_v"))
                s2_base = cur("s_t")
                gel_tgts = []
                for bi in range(16):
                    blk = half * 16 + bi
                    bsem = "s_ba" if bi % 2 == 0 else "s_bb"
                    if bi >= 2:
                        wait("sync", "s_t", s2_base + 4 * (bi - 1))
                    dma("sync", bsem, bd2_sb[bi % 2][:],
                        bd2_16[blk * 128:(blk + 1) * 128, :])
                    wait("tensor", bsem, cur(bsem))
                    bankA = pb[(bi % 2) * 2]
                    bankB = pb[(bi % 2) * 2 + 1]
                    if len(gel_tgts) >= 4:
                        wait("tensor", "s_s", gel_tgts[-3])
                    for sl4 in range(4):
                        bank = bankA if sl4 < 2 else bankB
                        off = (sl4 % 2) * TF
                        t_mm(bank[:, off:off + TF], bd2_sb[bi % 2][:, sl4 * 128:(sl4 + 1) * 128],
                             y_t[:, blk * TF:(blk + 1) * TF], start=True, stop=True)
                    mmd = cur("s_t")
                    wait("scalar", "s_t", mmd)
                    fl = bi * 4
                    gel_tgts.append(s_op(lambda e, bankA=bankA, fl=fl: e.activation(
                        h_half[:, fl * TF:(fl + 2) * TF], bankA[:, 0:2 * TF], AF.Gelu)))
                    gel_tgts.append(s_op(lambda e, bankB=bankB, fl=fl: e.activation(
                        h_half[:, (fl + 2) * TF:(fl + 4) * TF], bankB[:, 0:2 * TF], AF.Gelu)))
                s2_mm_end = cur("s_t")
                s2_done = cur("s_s")

                wait("tensor", "s_s", s2_done)
                wait("tensor", "s_v", cur("s_v"))
                cp_base = cur("s_t")
                for fg in range(8):
                    wcsem = "s_wca" if fg % 2 == 0 else "s_wcb"
                    if fg >= 2:
                        wait("sync", "s_t", cp_base + 64 * (fg - 1))
                    r0 = (half * 64 + fg * 8) * 128
                    dma("sync", wcsem, wc_sb[fg % 2][:],
                        wc16[r0:r0 + 1024, :].rearrange("(a p) c -> p a c", p=128))
                    wait("tensor", wcsem, cur(wcsem))
                    for fc8 in range(8):
                        fcl = fg * 8 + fc8
                        for o in range(8):
                            t_mm(pb[o][:, 0:TF],
                                 wc_sb[fg % 2][:, fc8 * 1024 + o * 128:fc8 * 1024 + o * 128 + 128],
                                 h_half[:, fcl * TF:(fcl + 1) * TF],
                                 start=(fcl == 0), stop=(fcl == 63))
                cp_done = cur("s_t")
                wait("vector", "s_t", cp_done)
                for o in range(8):
                    v_op(lambda e, o=o, c0=c0, bk=pb[o]: e.scalar_tensor_tensor(
                        acc[o][:, c0:c0 + TF], bk[:, 0:TF], 1.0, acc[o][:, c0:c0 + TF],
                        op0=ALU.mult, op1=ALU.add))
        barrier()

        ckpt('l0ffn')
        sb_free("bd2_sb1", "bd2_sb0", "bd1_sb", "wc_sb1", "wc_sb0", "h_half",
                "y_t", "xnb_t")

        wg = sb("wg", [128, 16 * 8], F32)

        bc1_a = sb("bc1_a", [128, T], F32)
        bc1_b = sb("bc1_b", [128, T], F32)
        sum1_r = sb("sum1_r", [1, T], F32)
        sq1_r = sb("sq1_r", [1, T], F32)
        xtmp = sb("xtmp", [128, 512], F32)
        sqt = [sb(f"sqt{i}", [128, 512], F32) for i in range(2)]
        ln1w_s = sb("ln1w_s", [128, 8], F32)
        ln1b_s = sb("ln1b_s", [128, 8], F32)
        ln2w_s = sb("ln2w_s", [128, 8], F32)
        ln2b_s = sb("ln2b_s", [128, 8], F32)
        mski = sb("mski", [128, 128], F32)
        mskl = sb("mskl", [128, 128], F32)
        tmask = sb("tmask_sb", [128, 8], F32)
        stg16 = [sb(f"stg16_{i}", [128, T], BF16) for i in range(2)]
        stg32 = [sb(f"stg32_{i}", [128, 512], F32) for i in range(2)]

        dma("sync", "s_c", ln1w_s[:], ln1w_in)
        dma("sync", "s_c", ln1b_s[:], ln1b_in)
        dma("sync", "s_c", ln2w_s[:], ln2w_in)
        dma("sync", "s_c", ln2b_s[:], ln2b_in)
        dma("sync", "s_c", mski[:], mski_in)
        dma("sync", "s_c", mskl[:], mskl_in)
        dma("sync", "s_c", tmask[:], tmask_in)
        ln_c = cur("s_c")
        wait("vector", "s_c", ln_c)
        wait("scalar", "s_c", ln_c)

        class Streamer:
            def __init__(self, name, shape, dtype, sems, cons_sem="s_t", n_buf=2):
                self.n_buf = n_buf
                self.tiles = [sb(f"{name}{i}", list(shape), dtype) for i in range(n_buf)]
                self.name = name
                self.sems = sems
                self.cons_sem = cons_sem
                self.i = 0
                self.cons = [0] * n_buf

            def load(self, src_ap):
                b = self.i % self.n_buf
                if self.cons[b]:
                    wait("sync", self.cons_sem, self.cons[b])
                tgt = dma("sync", self.sems[b], self.tiles[b][:], src_ap)
                self.i += 1
                return b, self.sems[b], tgt

            def consumed(self, b):
                self.cons[b] = cur(self.cons_sem)

            def free(self):
                sb_free(*[f"{self.name}{i}" for i in reversed(range(self.n_buf))])

        def _ln_rows_tail():
            v_op(lambda e: e.tensor_scalar(sum1_r[:], sum1_r[:], 1.0 / D, None, op0=ALU.mult))
            v_op(lambda e: e.tensor_scalar(sq1_r[:], sq1_r[:], 1.0 / D, None, op0=ALU.mult))
            v_op(lambda e: e.tensor_tensor(bc1_a[0:1, :], sum1_r[:], sum1_r[:], op=ALU.mult))
            v_op(lambda e: e.tensor_tensor(sq1_r[:], sq1_r[:], bc1_a[0:1, :], op=ALU.subtract))
            wait("scalar", "s_v", cur("s_v"))
            rs2 = s_op(lambda e: e.activation(sq1_r[:], sq1_r[:], AF.Sqrt, bias=eps_c[0:1, 0:1]))
            wait("vector", "s_s", rs2)
            v_op(lambda e: e.reciprocal(sq1_r[:], sq1_r[:]))
            v_op(lambda e: e.tensor_tensor(sum1_r[:], sum1_r[:], sq1_r[:], op=ALU.mult))
            v_op(lambda e: e.tensor_scalar(sum1_r[:], sum1_r[:], -1.0, None, op0=ALU.mult))
            rows_d = cur("s_v")
            wait("scalar", "s_v", rows_d)
            dma("scalar", "s_aux", row_scr[4:5, :], sq1_r[:])
            dma("scalar", "s_aux", row_scr[5:6, :], sum1_r[:])
            wait("scalar", "s_aux", cur("s_aux"))
            dma("scalar", "s_aux", bc1_a[:], row_scr[4:5, :].to_broadcast([128, T]))
            dma("scalar", "s_aux", bc1_b[:], row_scr[5:6, :].to_broadcast([128, T]))
            return cur("s_aux")

        def emit_ln_stats_sb(src):
            wait("tensor", "s_v", cur("s_v"))
            wait("tensor", "s_s", cur("s_s"))
            sqm2 = {}
            for sl in range(4):
                c0 = sl * 512
                for k in range(KD):
                    t_mm(pb[0][0:1, 0:512], ones_col[:], src[k][:, c0:c0 + 512],
                         start=(k == 0), stop=(k == KD - 1))
                for k in range(KD):
                    m = sl * KD + k
                    if m - 2 in sqm2:
                        wait("scalar", "s_t", sqm2[m - 2])
                    sq = s_op(lambda e, k=k, c0=c0: e.activation(
                        sqt[k % 2][:], src[k][:, c0:c0 + 512], AF.Square))
                    wait("tensor", "s_s", sq)
                    sqm2[m] = t_mm(pb[1][0:1, 0:512], ones_col[:], sqt[k % 2][:],
                                   start=(k == 0), stop=(k == KD - 1))
                mmd = cur("s_t")
                wait("vector", "s_t", mmd)
                v_op(lambda e, c0=c0, b0=pb[0]: e.tensor_copy(sum1_r[0:1, c0:c0 + 512], b0[0:1, 0:512]))
                v_op(lambda e, c0=c0, b1=pb[1]: e.tensor_copy(sq1_r[0:1, c0:c0 + 512], b1[0:1, 0:512]))
                wait("tensor", "s_v", cur("s_v"))
            return _ln_rows_tail()

        bc_s = sb("bc_s", [128, T], F32)
        bcd = emit_ln_stats_sb(acc)
        wait("vector", "s_aux", bcd)
        v_op(lambda e: e.tensor_tensor(sq1_r[:], sq1_r[:], sq1_r[:], op=ALU.mult))
        v_op(lambda e: e.tensor_scalar(sq1_r[:], sq1_r[:], -EPS, EPS + 1.0,
                                       op0=ALU.mult, op1=ALU.add))
        wait("scalar", "s_v", cur("s_v"))
        sqs = s_op(lambda e: e.activation(sq1_r[:], sq1_r[:], AF.Sqrt))
        wait("vector", "s_s", sqs)
        v_op(lambda e: e.reciprocal(sq1_r[:], sq1_r[:]))
        srow_d = cur("s_v")
        wait("scalar", "s_v", srow_d)
        dma("scalar", "s_aux", row_scr[6:7, :], sq1_r[:])
        wait("scalar", "s_aux", cur("s_aux"))
        dma("scalar", "s_aux", bc_s[:], row_scr[6:7, :].to_broadcast([128, T]))
        sb_done = cur("s_aux")

        wait("vector", "s_aux", sb_done)
        for k in range(KD):
            if k >= 1:
                wait("vector", "s_aux", cur("s_aux"))
            for sl in range(4):
                c0 = sl * 512
                v_op(lambda e, k=k, c0=c0: e.tensor_tensor(
                    xtmp[:], acc[k][:, c0:c0 + 512], bc1_a[:, c0:c0 + 512], op=ALU.mult))
                v_op(lambda e, k=k, c0=c0: e.tensor_tensor(
                    xtmp[:], xtmp[:], bc1_b[:, c0:c0 + 512], op=ALU.add))
                v_op(lambda e, k=k, c0=c0: e.tensor_copy(stg16[0][:, c0:c0 + 512], xtmp[:]))
                v_op(lambda e, k=k, c0=c0: e.tensor_tensor(
                    xtmp[:], xtmp[:], bc_s[:, c0:c0 + 512], op=ALU.mult))
                v_op(lambda e, k=k, c0=c0: e.scalar_tensor_tensor(
                    stg16[1][:, c0:c0 + 512], xtmp[:], ln1w_s[:, k:k + 1],
                    ln1b_s[:, k:k + 1].to_broadcast([128, 512]), op0=ALU.mult, op1=ALU.add))
            xnd = cur("s_v")
            wait("scalar", "s_v", xnd)
            dma("scalar", "s_aux", xn1_scr[k * 128:(k + 1) * 128, :], stg16[0][:])
            dma("scalar", "s_aux", ln1_scr[k * 128:(k + 1) * 128, :], stg16[1][:])
        xn_done = cur("s_aux")
        sb_free("bc_s")

        ckpt('p5b')
        barrier()
        kvw_sb = sb("kvw_sb", [128, 8 * 512], BF16)
        lnS = Streamer("lnSc", [128, 8 * 128], BF16, ("s_wa", "s_wb"))
        kvt16 = sb("kvt16", [128, 16 * 512], BF16)
        wait("sync", "s_cast", cur("s_cast"))
        dma("sync", "s_c", kvw_sb[:], qkv16d[:, 1024:1536].rearrange("(a p) c -> p a c", p=128))
        wait("sync", "s_aux", xn_done)
        wait("tensor", "s_c", cur("s_c"))
        for bch in range(16):
            b, sem, tgt = lnS.load(ln1_scr[:, bch * 128:(bch + 1) * 128]
                                   .rearrange("(a p) t -> p a t", p=128))
            wait("tensor", sem, tgt)
            bank = pb[4 + bch % 2]
            wait("tensor", "s_s", cur("s_s"))
            for k in range(KD):
                t_mm(bank[:, 0:512], lnS.tiles[b][:, k * 128:(k + 1) * 128],
                     kvw_sb[:, k * 512:(k + 1) * 512], start=(k == 0), stop=(k == KD - 1))
            lnS.consumed(b)
            mmd = cur("s_t")
            wait("scalar", "s_t", mmd)
            s_op(lambda e, bch=bch, bank=bank: e.copy(
                kvt16[:, bch * 512:(bch + 1) * 512], bank[:, 0:512]))
        kv_ev_done = cur("s_s")
        wait("gpsimd", "s_s", kv_ev_done)
        dma("gpsimd", "s_repb", kvloc.rearrange("(b p) c -> p b c", p=128), kvt16[:])
        wait("gpsimd", "s_repb", cur("s_repb"))
        if os.environ.get("KVG_PLAIN"):
            _inc("s_cc", 16)
            emit("gpsimd", lambda e: e.dma_start(
                out=kvg[0:T, :], in_=kvloc).then_inc(sems["s_cc"], 16))
        else:
            _inc("s_cc", 1)
            emit("gpsimd", lambda e: e.collective_compute(
                "AllGather", mybir.AluOpType.bypass,
                replica_groups=[list(range(8))],
                ins=[kvloc.opt()], outs=[kvg.opt()]).then_inc(sems["s_cc"], 1))
        kvg_done = cur("s_aux")
        sb_free("kvt16")
        lnS.free()
        sb_free("kvw_sb")

        ckpt('p5c')
        barrier()
        attn_cat = sb("attn_cat", [128, KD * T], BF16)
        vt16 = sb("vt16", [128, 16 * 256], BF16)
        wq_sb = sb("wq_sb", [128, 4 * 1024], BF16)
        wk_sb = sb("wk_sb", [128, 8 * 256], BF16)
        wv_sb = sb("wv_sb", [128, 8 * 256], BF16)
        xnS = Streamer("xnSd", [128, 8 * 512], BF16, ("s_ba", "s_bb"), n_buf=1)
        dma("sync", "s_c", wk_sb[:], wk16d.rearrange("(a p) c -> p a c", p=128))
        dma("sync", "s_c", wv_sb[:], wv16d.rearrange("(a p) c -> p a c", p=128))
        wait("tensor", "s_c", cur("s_c"))
        qpar = ("s_q0", "s_q1")
        wq_cons = 0
        for sl in range(4):
            b, sem, tgt = xnS.load(xn1_scr[:, sl * 512:(sl + 1) * 512]
                                   .rearrange("(a p) t -> p a t", p=128))
            wait("tensor", sem, tgt)
            for qh2 in range(2):
                if wq_cons:
                    wait("sync", "s_t", wq_cons)
                dma("sync", "s_wa", wq_sb[:],
                    wq16d[:, qh2 * 512:(qh2 + 1) * 512].rearrange("(a p) c -> p a c", p=128))
                wait("tensor", "s_wa", cur("s_wa"))
                for oci in range(4):
                    oc = qh2 * 4 + oci
                    bank = pb[4 + oc % 2]
                    wait("tensor", "s_s", cur("s_s"))
                    for k in range(KD):
                        t_mm(bank[:, 0:512],
                             wq_sb[:, k * 512 + oci * 128:k * 512 + oci * 128 + 128],
                             xnS.tiles[b][:, k * 512:(k + 1) * 512],
                             start=(k == 0), stop=(k == KD - 1))
                    mmd = cur("s_t")
                    wait("scalar", "s_t", mmd)
                    wait("scalar", qpar[oc % 2], cur(qpar[oc % 2]))
                    s_op(lambda e, oc=oc, bank=bank: e.activation(
                        stg16[oc % 2][:, 0:512], bank[:, 0:512], AF.Copy, scale=0.125))
                    wait("scalar", "s_s", cur("s_s"))
                    dma("scalar", qpar[oc % 2], qf_scr[oc * 128:(oc + 1) * 128, sl * 512:(sl + 1) * 512],
                        stg16[oc % 2][:, 0:512])
                wq_cons = cur("s_t")
            for kvh4 in range(4):
                bank = pb[4 + kvh4 % 2]
                wait("tensor", "s_s", cur("s_s"))
                for k in range(KD):
                    t_mm(bank[0:64, 0:512], wk_sb[:, k * 256 + kvh4 * 64:k * 256 + kvh4 * 64 + 64],
                         xnS.tiles[b][:, k * 512:(k + 1) * 512],
                         start=(k == 0), stop=(k == KD - 1))
                mmd = cur("s_t")
                wait("scalar", "s_t", mmd)
                wait("scalar", qpar[kvh4 % 2], cur(qpar[kvh4 % 2]))
                s_op(lambda e, kvh4=kvh4, bank=bank: e.copy(
                    stg16[kvh4 % 2][0:64, 0:512], bank[0:64, 0:512]))
                wait("scalar", "s_s", cur("s_s"))
                dma("scalar", qpar[kvh4 % 2], kf_scr[kvh4 * 64:(kvh4 + 1) * 64, sl * 512:(sl + 1) * 512],
                    stg16[kvh4 % 2][0:64, 0:512])
            for tci in range(4):
                tc = sl * 4 + tci
                bank = pb[4 + tci % 2]
                wait("tensor", "s_s", cur("s_s"))
                for k in range(KD):
                    t_mm(bank[:, 0:256], xnS.tiles[b][:, k * 512 + tci * 128:k * 512 + tci * 128 + 128],
                         wv_sb[:, k * 256:(k + 1) * 256], start=(k == 0), stop=(k == KD - 1))
                mmd = cur("s_t")
                wait("scalar", "s_t", mmd)
                s_op(lambda e, tc=tc, bank=bank: e.copy(
                    vt16[:, tc * 256:(tc + 1) * 256], bank[:, 0:256]))
            xnS.consumed(b)
        proj_done_s = cur("s_s")
        xnS.free()
        sb_free("wv_sb", "wk_sb", "wq_sb")

        ckpt('p5d')
        barrier()
        qhS = Streamer("qhSe", [64, T], BF16, ("s_wa", "s_wb"))
        kfS = Streamer("kfSe", [64, T], BF16, ("s_ba", "s_bb"))
        chi = sb("chi", [128, 16], BF16)
        clo = sb("clo", [128, 16], BF16)
        crows = sb("crows", [2, T], BF16)
        m_t = sb("m_t", [128, 1], F32)
        nm_t = sb("nm_t", [128, 1], F32)
        z_t = sb("z_t", [128, 1], F32)
        cc_t = sb("cc_t", [128, 1], F32)
        pt16 = [sb(f"pt16_{i}", [128, 512], BF16) for i in range(2)]
        ones2 = sb("ones2", [2, 128], BF16)
        v_op(lambda e: e.memset(ones2[:], 1.0))

        psum_free("pb3", "pb2", "pb1", "pb0")
        ps_s = psum("ps_s", [128, 2048], "left")

        wait("tensor", "s_s", proj_done_s)
        wait("scalar", "s_aux", cur("s_aux"))
        for h in range(16):
            kvh = h // 4
            qb_, qsem, qtgt = qhS.load(qf_scr[h * 64:(h + 1) * 64, :])
            wait("tensor", qsem, qtgt)
            qh = qhS.tiles[qb_]
            kb_, ksem, ktgt = kfS.load(kf_scr[kvh * 64:(kvh + 1) * 64, :])
            wait("tensor", ksem, ktgt)
            kfx = kfS.tiles[kb_]
            for i in range(16):
                tl = (i + 1) * 128
                wait("tensor", "s_v", cur("s_v"))
                wait("tensor", "s_s", cur("s_s"))
                for sl in range((tl + 511) // 512):
                    w_ = min(512, tl - sl * 512)
                    t_mm(ps_s[:, sl * 512:sl * 512 + w_],
                         qh[0:64, i * 128:(i + 1) * 128],
                         kfx[0:64, sl * 512:sl * 512 + w_],
                         start=True, stop=True)
                mm1 = cur("s_t")
                wait("vector", "s_t", mm1)
                v_op(lambda e, i=i: e.tensor_tensor(
                    ps_s[:, i * 128:(i + 1) * 128], ps_s[:, i * 128:(i + 1) * 128],
                    mski[:], op=ALU.add))
                vsync()
                v_op(lambda e, tl=tl: e.tensor_reduce(
                    m_t[:], ps_s[:, 0:tl], axis=AX.X, op=ALU.max))
                vsync()
                v_op(lambda e: e.tensor_scalar(nm_t[:], m_t[:], -1.0, None, op0=ALU.mult))
                vs = cur("s_v")
                wait("scalar", "s_v", vs)
                ex = s_op(lambda e, tl=tl: e.activation(
                    ps_s[:, 0:tl], ps_s[:, 0:tl], AF.Exp, bias=nm_t[:, 0:1],
                    accum_out=z_t[:, 0:1]))
                wait("scalar", "s_s", ex)
                ln = s_op(lambda e: e.activation(z_t[:], z_t[:], AF.Ln))
                wait("vector", "s_s", ln)
                v_op(lambda e: e.tensor_tensor(cc_t[:], m_t[:], z_t[:], op=ALU.add))
                vsync()
                v_op(lambda e: e.tensor_scalar(cc_t[:], cc_t[:], -1.0, None, op0=ALU.mult))
                vsync()
                v_op(lambda e, i=i: e.tensor_copy(chi[:, i:i + 1], cc_t[:]))
                vsync()
                v_op(lambda e, i=i: e.scalar_tensor_tensor(
                    clo[:, i:i + 1], chi[:, i:i + 1], -1.0, cc_t[:], op0=ALU.mult, op1=ALU.add))
                wait("tensor", "s_s", ex)
            st_done = cur("s_v")
            wait("scalar", "s_v", st_done)
            dma("scalar", "s_aux", crow_scr[0:1, 0:T].rearrange("a (c p) -> a p c", p=128),
                chi[:], slow=True)
            dma("scalar", "s_aux", crow_scr[1:2, 0:T].rearrange("a (c p) -> a p c", p=128),
                clo[:], slow=True)
            wait("scalar", "s_aux", cur("s_aux"))
            dma("scalar", "s_aux", crows[:, 0:T], crow_scr[:, 0:T])
            crow_d = cur("s_aux")

            for ss in range(4):
                ntc = ss * 4 + 4
                pvbank = pb[6 + h % 2]
                for tc in range(ntc):
                    ptb = pt16[tc % 2]
                    bank = pb[4 + tc % 2]
                    wait("tensor", "s_v", cur("s_v"))
                    t_mm(bank[:, 0:512],
                         kfx[0:64, tc * 128:(tc + 1) * 128],
                         qh[0:64, ss * 512:(ss + 1) * 512],
                         start=True, stop=False)
                    wait("tensor", "s_aux", crow_d)
                    t_mm(bank[:, 0:512], ones2[:, 0:128],
                         crows[:, ss * 512:(ss + 1) * 512], start=False, stop=True)
                    mm2 = cur("s_t")
                    wait("scalar", "s_t", mm2)
                    exv = s_op(lambda e, ptb=ptb, bank=bank: e.activation(
                        ptb[:], bank[:, 0:512], AF.Exp))
                    wait("vector", "s_s", exv)
                    if tc >= ss * 4:
                        sb_i = tc - ss * 4
                        if sb_i > 0:
                            v_op(lambda e, ptb=ptb, sb_i=sb_i: e.memset(
                                ptb[:, 0:sb_i * 128], 0.0))
                        v_op(lambda e, ptb=ptb, sb_i=sb_i: e.tensor_tensor(
                            ptb[:, sb_i * 128:(sb_i + 1) * 128],
                            ptb[:, sb_i * 128:(sb_i + 1) * 128], mskl[:], op=ALU.mult))
                    else:
                        v_op(lambda e: e.tensor_copy(m_t[:], m_t[:]))
                    ptd = cur("s_v")
                    wait("tensor", "s_v", ptd)
                    t_mm(pvbank[0:64, 0:512], vt16[:, tc * 256 + kvh * 64:tc * 256 + kvh * 64 + 64],
                         ptb[:], start=(tc == 0), stop=(tc == ntc - 1))
                pvd = cur("s_t")
                wait("scalar", "s_t", pvd)
                s_op(lambda e, h=h, ss=ss, pvbank=pvbank: e.copy(
                    attn_cat[(h % 2) * 64:(h % 2) * 64 + 64,
                             (h // 2) * T + ss * 512:(h // 2) * T + (ss + 1) * 512],
                    pvbank[0:64, 0:512]))
            qhS.consumed(qb_)
            kfS.consumed(kb_)
        attn_done_s = cur("s_s")

        sb_free("ones2", "pt16_1", "pt16_0", "cc_t", "z_t", "nm_t", "m_t",
                "crows", "clo", "chi", "kfSe1", "kfSe0", "qhSe1", "qhSe0")
        psum_free("ps_s")
        for i in range(4):
            pb[i] = psum(f"pbn{i}", [128, 512], "left")

        ckpt('p5e')
        barrier()
        sb_free("vt16")
        wo_sb = sb("wo_sb", [128, 8 * 1024], BF16)
        dma("sync", "s_c", wo_sb[:], wo16d.rearrange("(a p) c -> p a c", p=128))
        wait("tensor", "s_c", cur("s_c"))
        wait("tensor", "s_s", attn_done_s)
        for oc in range(KD):
            for sl in range(4):
                bankA = pb[0 if (oc * 4 + sl) % 2 == 0 else 2]
                bankB = pb[1 if (oc * 4 + sl) % 2 == 0 else 3]
                wait("tensor", "s_v", cur("s_v"))
                for par, bank in ((0, bankA), (1, bankB)):
                    hs = [h for h in range(16) if h % 2 == par]
                    for j, h in enumerate(hs):
                        t_mm(bank[:, 0:512],
                             wo_sb[par * 64:par * 64 + 64,
                                   (h // 2) * 1024 + oc * 128:(h // 2) * 1024 + oc * 128 + 128],
                             attn_cat[par * 64:par * 64 + 64,
                                      (h // 2) * T + sl * 512:(h // 2) * T + (sl + 1) * 512],
                             start=(j == 0), stop=(j == 7))
                mmd = cur("s_t")
                wait("vector", "s_t", mmd)
                v_op(lambda e, oc=oc, sl=sl, bank=bankA: e.scalar_tensor_tensor(
                    acc[oc][:, sl * 512:(sl + 1) * 512], bank[:, 0:512], 1.0,
                    acc[oc][:, sl * 512:(sl + 1) * 512], op0=ALU.mult, op1=ALU.add))
                v_op(lambda e, oc=oc, sl=sl, bank=bankB: e.scalar_tensor_tensor(
                    acc[oc][:, sl * 512:(sl + 1) * 512], bank[:, 0:512], 1.0,
                    acc[oc][:, sl * 512:(sl + 1) * 512], op0=ALU.mult, op1=ALU.add))
        sb_free("wo_sb", "attn_cat")

        ckpt('p5f')
        barrier()
        qkv_sb2 = sb("qkv_sb2", [128, 8 * 1024], BF16)
        lnG = Streamer("lnSg", [128, 8 * 128], BF16, ("s_wa", "s_wb"))
        kvs = sb("kvs", [128, 8 * 512], BF16)
        qb_t = sb("qb_t", [128, 1024], BF16)
        qk_t = sb("qk_t", [128, 1024], F32)
        sc_t = sb("sc_t", [128, 16 * 8], F32)
        sc_e = sb("sc_e", [128, 16 * 8], F32)
        mx8 = sb("mx8", [128, 16], F32)
        ao_t = sb("ao_t", [128, 1024], F32)
        dma("sync", "s_c", qkv_sb2[:], qkv16d[:, 0:1024].rearrange("(a p) c -> p a c", p=128))
        wait("vector", "s_cc", cur("s_cc"))
        wait("scalar", "s_cc", cur("s_cc"))
        wait("tensor", "s_c", cur("s_c"))
        for bch in range(16):
            b, sem, tgt = lnG.load(ln1_scr[:, bch * 128:(bch + 1) * 128]
                                   .rearrange("(a p) t -> p a t", p=128))
            wait("tensor", sem, tgt)
            wait("tensor", "s_v", cur("s_v"))
            for k in range(KD):
                for half in range(2):
                    t_mm(pb[4 + half][:, 0:512],
                         lnG.tiles[b][:, k * 128:(k + 1) * 128],
                         qkv_sb2[:, k * 1024 + half * 512:k * 1024 + (half + 1) * 512],
                         start=(k == 0), stop=(k == KD - 1))
            lnG.consumed(b)
            mmq = cur("s_t")
            wait("vector", "s_t", mmq)
            v_op(lambda e, b4=pb[4]: e.tensor_copy(qb_t[:, 0:512], b4[:, 0:512]))
            v_op(lambda e, b5=pb[5]: e.tensor_copy(qb_t[:, 512:1024], b5[:, 0:512]))
            if bch > 0:
                wait("scalar", "s_v", kvs_consumed)
            for tp in range(8):
                dma("scalar", "s_aux", kvs[:, tp * 512:(tp + 1) * 512],
                    kvg[tp * T + bch * 128:tp * T + bch * 128 + 128, :])
            kvld = cur("s_aux")
            wait("vector", "s_aux", kvld)
            for tp in range(8):
                kb = kvs[:, tp * 512:tp * 512 + 256].rearrange(
                    "p (g d) -> p g d", g=4).unsqueeze(2).to_broadcast([128, 4, 4, 64])
                v_op(lambda e, kb=kb: e.tensor_tensor(qk_t[:], qb_t[:], kb, op=ALU.mult))
                v_op(lambda e, tp=tp: e.tensor_reduce(
                    sc_t[:, tp:tp + 121:8],
                    qk_t.rearrange("p (h d) -> p h d", d=64), axis=AX.X, op=ALU.add))
            vsync()
            v_op(lambda e: e.tensor_scalar(sc_t[:], sc_t[:], 0.125, None, op0=ALU.mult))
            vsync()
            v_op(lambda e: e.tensor_tensor(
                sc_t.rearrange("p (h c) -> p h c", c=8),
                sc_t.rearrange("p (h c) -> p h c", c=8),
                tmask[:, :].unsqueeze(1).to_broadcast([128, 16, 8]), op=ALU.add))
            vsync()
            v_op(lambda e: e.tensor_reduce(
                mx8[:], sc_t.rearrange("p (h c) -> p h c", c=8), axis=AX.X, op=ALU.max))
            vsync()
            v_op(lambda e: e.tensor_tensor(
                sc_t.rearrange("p (h c) -> p h c", c=8),
                sc_t.rearrange("p (h c) -> p h c", c=8),
                mx8.rearrange("p (h o) -> p h o", o=1).to_broadcast([128, 16, 8]),
                op=ALU.subtract))
            vsm = cur("s_v")
            wait("scalar", "s_v", vsm)
            sex = s_op(lambda e: e.activation(sc_e[:], sc_t[:], AF.Exp))
            wait("vector", "s_s", sex)
            v_op(lambda e: e.tensor_reduce(
                mx8[:], sc_e.rearrange("p (h c) -> p h c", c=8), axis=AX.X, op=ALU.add))
            vsync()
            v_op(lambda e: e.reciprocal(mx8[:], mx8[:]))
            vsync()
            v_op(lambda e: e.tensor_tensor(
                sc_e.rearrange("p (h c) -> p h c", c=8),
                sc_e.rearrange("p (h c) -> p h c", c=8),
                mx8.rearrange("p (h o) -> p h o", o=1).to_broadcast([128, 16, 8]),
                op=ALU.mult))
            vsync()
            for tp in range(8):
                vb = kvs[:, tp * 512 + 256:tp * 512 + 512].rearrange(
                    "p (g d) -> p g d", g=4).unsqueeze(2).to_broadcast([128, 4, 4, 64])
                pbv = sc_e[:, tp:tp + 121:8].rearrange("p h -> p h 1" if False else "p (h o) -> p h o", o=1) \
                    .to_broadcast([128, 16, 64])
                if tp == 0:
                    v_op(lambda e, vb=vb, pbv=pbv: e.tensor_tensor(
                        ao_t.rearrange("p (h d) -> p h d", d=64), vb, pbv, op=ALU.mult))
                else:
                    v_op(lambda e, vb=vb, pbv=pbv: e.tensor_tensor(
                        qk_t.rearrange("p (h d) -> p h d", d=64), vb, pbv, op=ALU.mult))
                    v_op(lambda e: e.tensor_tensor(ao_t[:], ao_t[:], qk_t[:], op=ALU.add))
            kvs_consumed = cur("s_v")
            wait("vector", qpar[bch % 2], cur(qpar[bch % 2]))
            v_op(lambda e, bch=bch: e.tensor_copy(stg16[bch % 2][:, 0:1024], ao_t[:]))
            aod = cur("s_v")
            wait("scalar", "s_v", aod)
            dma("scalar", qpar[bch % 2], at_scr[bch * 128:(bch + 1) * 128, :], stg16[bch % 2][:, 0:1024])
        tiny_st_done = (cur("s_q0"), cur("s_q1"))
        sb_free("ao_t", "mx8", "sc_e", "sc_t", "qk_t", "qb_t", "kvs")
        lnG.free()
        sb_free("qkv_sb2")

        ckpt('p5g')
        barrier()
        atS = Streamer("atSh", [128, 1024], BF16, ("s_wa", "s_wb"))
        wait("sync", "s_q0", tiny_st_done[0])
        wait("sync", "s_q1", tiny_st_done[1])
        for bch in range(16):
            b, sem, tgt = atS.load(at_scr[bch * 128:(bch + 1) * 128, :])
            wait("tensor", sem, tgt)
            for oc in range(KD):
                bank = pb[4 + oc % 2]
                bank16 = bank.bitcast(BF16)
                wait("tensor", "s_s", cur("s_s"))
                t_tr(bank16[:, 0:128], atS.tiles[b][:, oc * 128:(oc + 1) * 128], bf=True)
                trd = cur("s_t")
                wait("scalar", "s_t", trd)
                wait("scalar", qpar[oc % 2], cur(qpar[oc % 2]))
                s_op(lambda e, oc=oc, bank16=bank16: e.copy(
                    stg16[oc % 2][:, 0:128], bank16[:, 0:128]))
                wait("scalar", "s_s", cur("s_s"))
                dma("scalar", qpar[oc % 2], af_scr[oc * 128:(oc + 1) * 128, bch * 128:(bch + 1) * 128],
                    stg16[oc % 2][:, 0:128])
            atS.consumed(b)
        af_done = (cur("s_q0"), cur("s_q1"))
        atS.free()

        ckpt('p5h')
        barrier()
        prj_sb = sb("prj_sb", [128, 8 * 1024], BF16)
        afS = Streamer("afSi", [128, 8 * 512], BF16, ("s_ba", "s_bb"))
        xnI = Streamer("xnIi", [128, 8 * 512], BF16, ("s_sa", "s_sb"), cons_sem="s_v")
        dma("sync", "s_c", prj_sb[:], proj16d.rearrange("(a p) c -> p a c", p=128))
        wait("sync", "s_q0", af_done[0])
        wait("sync", "s_q1", af_done[1])
        wait("tensor", "s_c", cur("s_c"))
        for sl in range(4):
            b, sem, tgt = afS.load(af_scr[:, sl * 512:(sl + 1) * 512]
                                   .rearrange("(a p) t -> p a t", p=128))
            bx, semx, tgtx = xnI.load(xn1_scr[:, sl * 512:(sl + 1) * 512]
                                      .rearrange("(a p) t -> p a t", p=128))
            wait("tensor", sem, tgt)
            wait("vector", semx, tgtx)
            for oc in range(KD):
                bank = pb[4 + oc % 2]
                wait("tensor", "s_v", cur("s_v"))
                for k in range(KD):
                    t_mm(bank[:, 0:512], prj_sb[:, k * 1024 + oc * 128:k * 1024 + oc * 128 + 128],
                         afS.tiles[b][:, k * 512:(k + 1) * 512],
                         start=(k == 0), stop=(k == KD - 1))
                mmd = cur("s_t")
                wait("vector", "s_t", mmd)
                wait("vector", qpar[oc % 2], cur(qpar[oc % 2]))
                v_op(lambda e, oc=oc, bank=bank, bx=bx: e.tensor_tensor(
                    stg32[oc % 2][:], bank[:, 0:512], xnI.tiles[bx][:, oc * 512:(oc + 1) * 512],
                    op=ALU.add))
                h2d = cur("s_v")
                wait("scalar", "s_v", h2d)
                dma("scalar", qpar[oc % 2], h2_scr[oc * 128:(oc + 1) * 128, sl * 512:(sl + 1) * 512],
                    stg32[oc % 2][:])
            afS.consumed(b)
            xnI.cons[bx] = cur("s_v")
        h2_done = (cur("s_q0"), cur("s_q1"))
        xnI.free()
        afS.free()
        sb_free("prj_sb")

        ckpt('p5i')
        barrier()
        h2S = Streamer("h2Sj", [128, 512], F32, ("s_wa", "s_wb"))
        wait("sync", "s_q0", h2_done[0])
        wait("sync", "s_q1", h2_done[1])
        wait("tensor", "s_v", cur("s_v"))
        wait("tensor", "s_s", cur("s_s"))
        sqm3 = {}
        for sl in range(4):
            c0 = sl * 512
            for k in range(KD):
                m = sl * KD + k
                b, sem, tgt = h2S.load(h2_scr[k * 128:(k + 1) * 128, c0:c0 + 512])
                wait("tensor", sem, tgt)
                t_mm(pb[0][0:1, 0:512], ones_col[:], h2S.tiles[b][:],
                     start=(k == 0), stop=(k == KD - 1))
                if m - 2 in sqm3:
                    wait("scalar", "s_t", sqm3[m - 2])
                wait("scalar", sem, tgt)
                sq = s_op(lambda e, b=b, k=k: e.activation(
                    sqt[k % 2][:], h2S.tiles[b][:], AF.Square))
                wait("tensor", "s_s", sq)
                sqm3[m] = t_mm(pb[1][0:1, 0:512], ones_col[:], sqt[k % 2][:],
                               start=(k == 0), stop=(k == KD - 1))
                h2S.consumed(b)
            mmd = cur("s_t")
            wait("vector", "s_t", mmd)
            v_op(lambda e, c0=c0, b0=pb[0]: e.tensor_copy(sum1_r[0:1, c0:c0 + 512], b0[0:1, 0:512]))
            v_op(lambda e, c0=c0, b1=pb[1]: e.tensor_copy(sq1_r[0:1, c0:c0 + 512], b1[0:1, 0:512]))
            wait("tensor", "s_v", cur("s_v"))
        bcd2 = _ln_rows_tail()
        h2F = Streamer("h2Fj", [128, T], F32, ("s_ba", "s_bb"), cons_sem="s_v")
        wait("vector", "s_aux", bcd2)
        for k in range(KD):
            b, sem, tgt = h2F.load(h2_scr[k * 128:(k + 1) * 128, :])
            wait("vector", sem, tgt)
            wait("vector", qpar[k % 2], cur(qpar[k % 2]))
            for sl in range(4):
                c0 = sl * 512
                v_op(lambda e, b=b, c0=c0: e.tensor_tensor(
                    xtmp[:], h2F.tiles[b][:, c0:c0 + 512], bc1_a[:, c0:c0 + 512], op=ALU.mult))
                v_op(lambda e, c0=c0: e.tensor_tensor(
                    xtmp[:], xtmp[:], bc1_b[:, c0:c0 + 512], op=ALU.add))
                v_op(lambda e, k=k, c0=c0: e.scalar_tensor_tensor(
                    stg16[k % 2][:, c0:c0 + 512], xtmp[:], ln2w_s[:, k:k + 1],
                    ln2b_s[:, k:k + 1].to_broadcast([128, 512]), op0=ALU.mult, op1=ALU.add))
            v_op(lambda e, k=k, b=b: e.tensor_tensor(acc[k][:], acc[k][:], h2F.tiles[b][:],
                                                     op=ALU.add))
            h2F.consumed(b)
            lnd = cur("s_v")
            wait("scalar", "s_v", lnd)
            dma("scalar", qpar[k % 2], ln2_scr[k * 128:(k + 1) * 128, :], stg16[k % 2][:])
        ln2_done_aux = (cur("s_q0"), cur("s_q1"))
        h2F.free()
        h2S.free()

        ckpt('p5j')
        barrier()
        rt_sb = sb("rt_sb", [128, 8 * 8], BF16)
        lnR = Streamer("lnRk", [128, 8 * 128], BF16, ("s_wa", "s_wb"))
        gmax = sb("gmax", [128, 1], F32)
        g2t = sb("g2t", [128, 8], F32)
        dma("sync", "s_c", rt_sb[:], rout16d.rearrange("(a p) c -> p a c", p=128))
        wait("sync", "s_q0", ln2_done_aux[0])
        wait("sync", "s_q1", ln2_done_aux[1])
        wait("tensor", "s_c", cur("s_c"))
        for bch in range(16):
            b, sem, tgt = lnR.load(ln2_scr[:, bch * 128:(bch + 1) * 128]
                                   .rearrange("(a p) t -> p a t", p=128))
            wait("tensor", sem, tgt)
            bank = pb[4 + bch % 2]
            wait("tensor", "s_v", cur("s_v"))
            for k in range(KD):
                t_mm(bank[:, 0:8], lnR.tiles[b][:, k * 128:(k + 1) * 128],
                     rt_sb[:, k * 8:(k + 1) * 8], start=(k == 0), stop=(k == KD - 1))
            lnR.consumed(b)
            mmd = cur("s_t")
            wait("vector", "s_t", mmd)
            v_op(lambda e, bch=bch, bank=bank: e.tensor_copy(
                wg[:, bch * 8:(bch + 1) * 8], bank[:, 0:8]))
        for bch in range(16):
            seg = wg[:, bch * 8:(bch + 1) * 8]
            v_op(lambda e, seg=seg: e.tensor_reduce(gmax[:], seg, axis=AX.X, op=ALU.max))
            vsync()
            v_op(lambda e, seg=seg: e.tensor_scalar(
                seg, seg, gmax[:, 0:1], None, op0=ALU.subtract))
            vsm = cur("s_v")
            wait("scalar", "s_v", vsm)
            sx = s_op(lambda e, seg=seg: e.activation(seg, seg, AF.Exp,
                                                      accum_out=gmax[:, 0:1]))
            wait("vector", "s_s", sx)
            v_op(lambda e: e.reciprocal(gmax[:], gmax[:]))
            vsync()
            v_op(lambda e, seg=seg: e.tensor_scalar(
                seg, seg, gmax[:, 0:1], None, op0=ALU.mult))
            vsync()
            v_op(lambda e, seg=seg: e.tensor_reduce(gmax[:], seg, axis=AX.X, op=ALU.max))
            vsync()
            v_op(lambda e, seg=seg: e.tensor_scalar(
                g2t[:], seg, gmax[:, 0:1], None, op0=ALU.is_lt))
            vsync()
            v_op(lambda e, seg=seg: e.tensor_tensor(g2t[:], seg, g2t[:], op=ALU.mult))
            vsync()
            v_op(lambda e: e.tensor_reduce(gmax[:], g2t[:], axis=AX.X, op=ALU.max))
            vsync()
            v_op(lambda e, seg=seg: e.tensor_scalar(
                g2t[:], seg, gmax[:, 0:1], None, op0=ALU.is_ge))
            vsync()
            v_op(lambda e, seg=seg: e.tensor_tensor(seg, seg, g2t[:], op=ALU.mult))
            vsync()
        gate_done = cur("s_v")
        sb_free("g2t", "gmax")
        lnR.free()
        sb_free("rt_sb")

        sb_free("stg32_1", "stg32_0", "stg16_1", "stg16_0", "tmask_sb", "mskl", "mski",
                "ln2b_s", "ln2w_s", "ln1b_s", "ln1w_s", "sqt1", "sqt0", "xtmp",
                "sq1_r", "sum1_r", "bc1_b", "bc1_a")

        ckpt('p5k')
        barrier()
        w1_sb = sb("w1_sb", [128, 8 * 2048], BF16)
        w2_sb = sb("w2_sb", [128, 16 * 1024], BF16)
        he16 = sb("he16", [128, 16 * 512], BF16)
        atok = sb("atok", [128, 4 * 1024], F32)
        ln2q = sb("ln2q", [128, 8 * 512], BF16)

        wait("tensor", "s_v", gate_done)
        w1_consumed = 0
        w2_consumed = 0
        ln2q_consumed = 0
        for tq in range(4):
            q0 = tq * 512
            wait("vector", "s_o0", cur("s_o0"))
            wait("vector", "s_o1", cur("s_o1"))
            if ln2q_consumed:
                wait("sync", "s_t", ln2q_consumed)
            lq = dma("sync", "s_ba", ln2q[:],
                     ln2_scr[:, q0:q0 + 512].rearrange("(a p) t -> p a t", p=128))
            wait("tensor", "s_ba", cur("s_ba"))
            for e_i in range(8):
                wsem = "s_wa" if e_i % 2 == 0 else "s_wb"
                if w1_consumed:
                    wait("sync", "s_t", w1_consumed)
                dma("sync", wsem, w1_sb[:],
                    w116d[e_i * 1024:(e_i + 1) * 1024, :].rearrange("(a p) c -> p a c", p=128))
                if w2_consumed:
                    wait("sync", "s_t", w2_consumed)
                dma("sync", wsem, w2_sb[:],
                    w216d[e_i * 2048:(e_i + 1) * 2048, :].rearrange("(a p) c -> p a c", p=128))
                wld = cur(wsem)
                wait("tensor", wsem, wld)
                ge_tgts = []
                for f in range(16):
                    bank = pb[4 + f % 2]
                    if len(ge_tgts) >= 2:
                        wait("tensor", "s_s", ge_tgts[-2])
                    for k in range(KD):
                        t_mm(bank[:, 0:512], w1_sb[:, k * 2048 + f * 128:k * 2048 + f * 128 + 128],
                             ln2q[:, k * 512:(k + 1) * 512],
                             start=(k == 0), stop=(k == KD - 1))
                    mmd = cur("s_t")
                    wait("scalar", "s_t", mmd)
                    ge_tgts.append(s_op(lambda e, f=f, bank=bank: e.activation(
                        he16[:, f * 512:(f + 1) * 512], bank[:, 0:512], AF.Gelu_apprx_tanh)))
                w1_consumed = cur("s_t")
                wait("tensor", "s_s", cur("s_s"))
                for ts_i in range(4):
                    wait("tensor", "s_v", cur("s_v"))
                    for f in range(16):
                        for half in range(2):
                            t_mm(pb[6 + half][:, 0:512],
                                 he16[:, f * 512 + ts_i * 128:f * 512 + ts_i * 128 + 128],
                                 w2_sb[:, f * 1024 + half * 512:f * 1024 + (half + 1) * 512],
                                 start=(f == 0), stop=(f == 15))
                    mmd = cur("s_t")
                    wait("vector", "s_t", mmd)
                    wcol = wg[:, (tq * 4 + ts_i) * 8 + e_i:(tq * 4 + ts_i) * 8 + e_i + 1]
                    for half in range(2):
                        if e_i == 0:
                            v_op(lambda e, ts_i=ts_i, half=half, wcol=wcol, bk=pb[6 + half]:
                                 e.tensor_scalar(
                                atok[:, ts_i * 1024 + half * 512:ts_i * 1024 + (half + 1) * 512],
                                bk[:, 0:512], wcol, None, op0=ALU.mult))
                        else:
                            v_op(lambda e, ts_i=ts_i, half=half, wcol=wcol, bk=pb[6 + half]:
                                 e.scalar_tensor_tensor(
                                atok[:, ts_i * 1024 + half * 512:ts_i * 1024 + (half + 1) * 512],
                                bk[:, 0:512], wcol,
                                atok[:, ts_i * 1024 + half * 512:ts_i * 1024 + (half + 1) * 512],
                                op0=ALU.mult, op1=ALU.add))
                w2_consumed = cur("s_t")
            ln2q_consumed = cur("s_t")

            for ts_i in range(4):
                osem = "s_o0" if ts_i % 2 == 0 else "s_o1"
                for oc in range(KD):
                    bank = pb[4 + oc % 2]
                    wait("tensor", "s_v", cur("s_v"))
                    t_tr(bank[:, 0:128], acc[oc][:, q0 + ts_i * 128:q0 + (ts_i + 1) * 128])
                    trd = cur("s_t")
                    wait("vector", "s_t", trd)
                    v_op(lambda e, ts_i=ts_i, oc=oc, bank=bank: e.scalar_tensor_tensor(
                        atok[:, ts_i * 1024 + oc * 128:ts_i * 1024 + oc * 128 + 128],
                        bank[:, 0:128], 1.0,
                        atok[:, ts_i * 1024 + oc * 128:ts_i * 1024 + oc * 128 + 128],
                        op0=ALU.mult, op1=ALU.add))
                std = cur("s_v")
                wait("sync", "s_v", std)
                dma("sync", osem, out_hbm[q0 + ts_i * 128:q0 + (ts_i + 1) * 128, :],
                    atok[:, ts_i * 1024:(ts_i + 1) * 1024])
        wait("sync", "s_o0", cur("s_o0"))
        wait("sync", "s_o1", cur("s_o1"))


    except _StopEmission:
        barrier()
        if DUMP == "acc":
            for k in range(KD):
                dma("sync", "s_o0", out_hbm[k * 256:(k + 1) * 256, :]
                    .rearrange("(a b) c -> a (b c)", a=128), acc[k][:])
            wait("sync", "s_o0", cur("s_o0"))
        elif DUMP == "tmask" and "tmask_sb" in ctx:
            dma("sync", "s_o0", out_hbm[0:128, 0:8], tmask[:])
            wait("sync", "s_o0", cur("s_o0"))
        elif DUMP is not None:
            src = {"xn1_scr": xn1_scr, "ln1_scr": ln1_scr, "qf_scr": qf_scr,
                   "kf_scr": kf_scr, "kvloc": kvloc, "kvg": kvg, "at_scr": at_scr,
                   "af_scr": af_scr, "h2_scr": h2_scr, "ln2_scr": ln2_scr}[DUMP]
            if DUMP == "kvg" and cur("s_cc") > 0:
                wait("gpsimd", "s_cc", cur("s_cc"))
            n = 1
            for d in src.shape:
                n *= d
            n = min(n, T * D)
            dma("gpsimd", "s_o0", out_hbm.rearrange("a b -> (a b)")[0:n],
                src.rearrange("a b -> (a b)")[0:n])
            wait("sync", "s_o0", cur("s_o0"))

    with nc.Block() as block:
        @block.sync
        def _(e):
            for f in ops["sync"]:
                f(e)

        @block.scalar
        def _(e):
            for f in ops["scalar"]:
                f(e)

        @block.vector
        def _(e):
            for f in ops["vector"]:
                f(e)

        @block.tensor
        def _(e):
            for f in ops["tensor"]:
                f(e)

        @block.gpsimd
        def _(e):
            for f in ops["gpsimd"]:
                f(e)

    return nc


def kernel(**inp):
    w = host_layout(inp)
    x = np.asarray(inp["L0_attn_hidden_states"], np.float32)
    nc = build_program()
    in_maps = []
    for c in range(8):
        m = dict(w)
        m["x0T"] = np.ascontiguousarray(x[c].T)
        tm = np.zeros((128, 8), np.float32)
        tm[:, c + 1:] = -1e30
        m["tmask"] = tm
        in_maps.append(m)
    res = run_bass_kernel_spmd(nc, in_maps, core_ids=list(range(8)))
    out = np.stack([r["out"] for r in res.results], axis=0)
    return out.astype(np.float32)
